# version 02
# baseline (speedup 1.0000x reference)
import numpy as np
import ml_dtypes
from contextlib import ExitStack

import concourse.bass as bass
import concourse.mybir as mybir
from concourse.bass_utils import run_bass_kernel_spmd

F32 = mybir.dt.float32
BF16 = mybir.dt.bfloat16
I32 = mybir.dt.int32
AF = mybir.ActivationFunctionType
ALU = mybir.AluOpType
AX = mybir.AxisListType
BF16NP = ml_dtypes.bfloat16

NCORES = 8
B_PER_CORE = 2
S = 2048
D = 1024
TOK = B_PER_CORE * S
NT = TOK // 128
NT_B = S // 128
DC = D // 128
LN_EPS = 1e-5


class Eng:
    def __init__(self, name):
        self.name = name
        self.e = None
        self.sem = None
        self.count = 0
        self.waited = {}

    def wait(self, *tokens):
        for t in tokens:
            if t is None:
                continue
            sem, val = t
            if val <= 0:
                continue
            key = id(sem)
            if self.waited.get(key, -1) >= val:
                continue
            self.e.wait_ge(sem, val)
            self.waited[key] = val

    def inc(self, inst, dma=False, psem=None):
        if psem is not None:
            return psem.inc(inst)
        step = 16 if dma else 1
        self.count += step
        inst.then_inc(self.sem, step)
        return (self.sem, self.count)

    def tok(self):
        return (self.sem, self.count)


class PSem:

    def __init__(self, handle):
        self.h = handle
        self.count = 0

    def inc(self, inst, step=1):
        self.count += step
        inst.then_inc(self.h, step)
        return (self.h, self.count)

    def at(self, val):
        return (self.h, val)


class KCtx:
    def __init__(self, nc, block):
        self.nc = nc
        self.block = block
        self.te = Eng("te")
        self.ve = Eng("ve")
        self.se = Eng("se")
        self.po = Eng("po")
        self.sq = Eng("sq")
        self.engines = [self.te, self.ve, self.se, self.po, self.sq]
        self.psems = []
        self.extra_psems = []
        self._psem_next = 0

    def attach(self, es):
        nc = self.nc
        for e in self.engines:
            e.sem = es.enter_context(nc.semaphore("sem_" + e.name))
        for i in range(24):
            self.psems.append(PSem(es.enter_context(nc.semaphore(f"ps{i}"))))
        self.dsems_q = [PSem(es.enter_context(nc.semaphore(f"dq{i}")))
                        for i in range(16)]
        self.dsems_p = [PSem(es.enter_context(nc.semaphore(f"dp{i}")))
                        for i in range(8)]
        self._dsem_q_next = 0
        self._dsem_p_next = 0

    def psem(self):
        p = self.psems[self._psem_next % len(self.psems)]
        self._psem_next += 1
        return p

    def named_psems(self, es, prefix, names):
        d = {n: PSem(es.enter_context(
            self.nc.semaphore(f"{prefix}_{n}"))) for n in names}
        self.extra_psems.extend(d.values())
        return d

    def dsem(self, queue="q"):
        if queue == "q":
            p = self.dsems_q[self._dsem_q_next % len(self.dsems_q)]
            self._dsem_q_next += 1
        else:
            p = self.dsems_p[self._dsem_p_next % len(self.dsems_p)]
            self._dsem_p_next += 1
        return p

    def V(self, fn):
        blk = self.block

        def run(v):
            self.ve.e = v
            fn(v)
        blk.vector(run)

    def Sx(self, fn):
        def run(s):
            self.se.e = s
            fn(s)
        self.block.scalar(run)

    def T(self, fn):
        def run(t):
            self.te.e = t
            fn(t)
        self.block.tensor(run)

    def G(self, fn):
        def run(g):
            self.po.e = g
            fn(g)
        self.block.gpsimd(run)

    def Q(self, fn):
        def run(q):
            self.sq.e = q
            fn(q)
        self.block.sync(run)

    def barrier(self):
        toks = [e.tok() for e in self.engines]
        toks += [(p.h, p.count) for p in self.psems]
        toks += [(p.h, p.count) for p in self.extra_psems]
        toks += [(p.h, p.count) for p in self.dsems_q]
        toks += [(p.h, p.count) for p in self.dsems_p]
        self.T(lambda t: self.te.wait(*toks))
        self.V(lambda v: self.ve.wait(*toks))
        self.Sx(lambda s: self.se.wait(*toks))
        self.G(lambda g: self.po.wait(*toks))
        self.Q(lambda q: self.sq.wait(*toks))



def ln_stats_v(K, v, x_ap, st8):
    n = x_ap.shape[-1]
    K.ve.wait(K.ve.tok())
    t1 = K.ve.inc(v.reduce_sum(st8[:, 0:1], x_ap, axis=AX.X))
    t3 = K.ve.inc(v.scalar_tensor_tensor(K.sqscr[:, 0:n], x_ap, 1.0, x_ap,
                                         op0=ALU.mult, op1=ALU.mult,
                                         accum_out=st8[:, 2:3]))
    K.ve.wait(t1, t3)
    t2 = K.ve.inc(v.tensor_scalar(st8[:, 1:2], st8[:, 0:1], 1.0 / n, None,
                                  op0=ALU.mult))
    K.ve.wait(t2)
    t4 = K.ve.inc(v.tensor_scalar(st8[:, 5:6], st8[:, 1:2], st8[:, 1:2],
                                  None, op0=ALU.mult))
    K.ve.wait(t4)
    t5 = K.ve.inc(v.tensor_scalar(st8[:, 3:4], st8[:, 2:3], 1.0 / n,
                                  st8[:, 5:6], op0=ALU.mult, op1=ALU.subtract))
    return t5


def ln_apply_s(K, s, x_ap, out_ap, st8, want_norm=False, extra_psem=None):
    n = x_ap.shape[-1]
    t6 = K.se.inc(s.activation(st8[:, 4:5], st8[:, 3:4], AF.Ln,
                               bias=K.eps_ap, scale=1.0))
    K.se.wait(t6)
    t7 = K.se.inc(s.activation(st8[:, 4:5], st8[:, 4:5], AF.Exp,
                               bias=K.zero_ap, scale=-0.5))
    K.se.wait(t7)
    t8 = K.se.inc(s.mul(st8[:, 6:7], st8[:, 1:2], st8[:, 4:5]))
    K.se.wait(t8)
    t9 = K.se.inc(s.mul(st8[:, 6:7], st8[:, 6:7], -1.0))
    K.se.wait(t9)
    inst = s.activation(out_ap, x_ap, AF.Identity, bias=st8[:, 6:7],
                        scale=st8[:, 4:5])
    tA = K.se.inc(inst, psem=extra_psem)
    tN = None
    if want_norm:
        tb = K.se.inc(s.square(st8[:, 5:6], st8[:, 4:5]))
        K.se.wait(tb)
        tc = K.se.inc(s.mul(st8[:, 5:6], st8[:, 5:6], float(n)))
        K.se.wait(tc)
        td = K.se.inc(s.mul(st8[:, 5:6], st8[:, 5:6], st8[:, 3:4]))
        K.se.wait(td)
        tN = K.se.inc(s.activation(st8[:, 7:8], st8[:, 5:6], AF.Sqrt,
                                   bias=K.zero_ap, scale=1.0))
    return tA, tN


def emit_commons(K, es, cst):
    nc = K.nc
    K.identf = es.enter_context(nc.sbuf_tensor("c_idf", [128, 128], F32))
    K.ident = es.enter_context(nc.sbuf_tensor("c_id", [128, 128], BF16))
    K.ones_f32 = es.enter_context(nc.sbuf_tensor("c_onesf", [1, 128], F32))
    K.ones_bf = es.enter_context(nc.sbuf_tensor("c_onesb", [1, 512], BF16))
    K.ones_col = es.enter_context(nc.sbuf_tensor("c_onesc", [128, 1], BF16))
    K.zrow16_t = es.enter_context(nc.sbuf_tensor("c_zrow", [1, 16], BF16))
    K.consts = es.enter_context(nc.sbuf_tensor("c_eps", [128, 2], F32))
    K.sqscr = es.enter_context(nc.sbuf_tensor("c_sqscr", [128, 1024], F32))
    K.eps_ap = K.consts[:, 0:1]
    K.zero_ap = K.consts[:, 1:2]

    def loads(q):
        K.sq.inc(q.dma_start(out=K.identf[:], in_=cst[0:128, :]), dma=True)
        t = K.sq.inc(q.dma_start(out=K.ones_f32[:], in_=cst[128:129, :]),
                     dma=True)
        K.sq.wait(t)
    K.Q(loads)

    def prep(v):
        K.ve.wait(K.sq.tok())
        K.ve.inc(v.tensor_copy(K.ident[:], K.identf[:]))
        K.ve.inc(v.memset(K.ones_bf[:], 1.0))
        K.ve.inc(v.memset(K.ones_col[:], 1.0))
        K.ve.inc(v.memset(K.zrow16_t[:], 0.0))
        K.ve.inc(v.memset(K.consts[:, 0:1], LN_EPS))
        t = K.ve.inc(v.memset(K.consts[:, 1:2], 0.0))
        K.ve.wait(t)
    K.V(prep)



def emit_l0_attn(K, es, x_in, x_out, upto=99):
    nc = K.nc
    xt = [es.enter_context(nc.sbuf_tensor(f"l0a_xt{i}", [128, D], F32))
          for i in range(2)]
    x2 = [es.enter_context(nc.sbuf_tensor(f"l0a_x2{i}", [128, D], F32))
          for i in range(2)]
    xl = es.enter_context(nc.sbuf_tensor("l0a_xl", [128, NT_B * D], BF16))
    xfm = es.enter_context(nc.sbuf_tensor("l0a_xfm", [128, DC * S], BF16))
    stt = es.enter_context(nc.sbuf_tensor("l0a_st", [128, 16], F32))
    nrm = es.enter_context(nc.sbuf_tensor("l0a_nrm", [128, NT_B], F32))
    cneg = es.enter_context(nc.sbuf_tensor("l0a_cneg", [128, NT_B], F32))
    zz = es.enter_context(nc.sbuf_tensor("l0a_zz", [128, NT_B], F32))
    zpart = es.enter_context(nc.sbuf_tensor("l0a_zp", [128, 4], F32))
    mrow = es.enter_context(nc.sbuf_tensor("l0a_mrow", [1, 128], F32))
    mcol = es.enter_context(nc.sbuf_tensor("l0a_mcol", [128, 1], F32))
    eall = es.enter_context(nc.sbuf_tensor("l0a_eall", [128, NT_B * 512],
                                           BF16))
    esc = es.enter_context(nc.sbuf_tensor("l0a_esc", [128, 512], F32))
    yt = es.enter_context(nc.sbuf_tensor("l0a_yt", [128, 2 * D], F32))
    nt_ = es.enter_context(nc.sbuf_tensor("l0a_nt", [128, 2 * D], F32))
    ot = [es.enter_context(nc.sbuf_tensor(f"l0a_ot{i}", [128, D], F32))
          for i in range(2)]
    psS = [es.enter_context(nc.psum_tensor(f"l0a_psS{i}", [128, 512], F32))
           for i in range(2)]
    psPV = [es.enter_context(nc.psum_tensor(f"l0a_pv{i}", [128, D], F32))
            for i in range(2)]
    pst = [es.enter_context(nc.psum_tensor(f"l0a_pst{i}", [128, 512], BF16))
           for i in range(2)]
    psM = psS[0]

    for b in range(B_PER_CORE):
        p_cons = K.psem()
        p_evac = K.psem()
        cons_b = p_cons.count
        evac_b = p_evac.count
        dx = [K.dsem(), K.dsem()]
        dx_b = [d.count for d in dx]
        ld = []

        def A_loads(q, b=b, ld=ld, cons_b=cons_b, p_cons=p_cons):
            for t in range(NT_B):
                if t >= 2:
                    K.sq.wait(p_cons.at(cons_b + (t - 2) + 1))
                r0 = b * S + t * 128
                ld.append(dx[t % 2].inc(
                    q.dma_start(out=xt[t % 2][:], in_=x_in[r0:r0 + 128, :]),
                    step=16))
        K.Q(A_loads)

        var_toks = []

        def A_vec(v, b=b, p_cons=p_cons, cons_b=cons_b):
            for t in range(NT_B):
                K.ve.wait(ld[t])
                if t >= 2:
                    K.ve.wait(p_cons.at(cons_b + (t - 2) + 1))
                st8 = stt[:, (t % 2) * 8:(t % 2) * 8 + 8]
                var_toks.append(ln_stats_v(K, v, xt[t % 2][:], st8))
        K.V(A_vec)

        ap_toks = []

        def A_sc(s, b=b, p_cons=p_cons, cons_b=cons_b):
            for t in range(NT_B):
                K.se.wait(var_toks[t])
                st8 = stt[:, (t % 2) * 8:(t % 2) * 8 + 8]
                tA, tN = ln_apply_s(K, s, xt[t % 2][:],
                                    xl[:, t * D:(t + 1) * D], st8,
                                    want_norm=True)
                K.se.wait(tN)
                inst = s.copy(nrm[:, t:t + 1], st8[:, 7:8])
                K.se.inc(inst, psem=p_cons)
                ap_toks.append(tA)
        K.Sx(A_sc)

        tr_toks = []

        def A_tr(teng, b=b, evac_b=evac_b, p_evac=p_evac):
            for t in range(NT_B):
                K.te.wait(ap_toks[t])
                for half in range(2):
                    r = t * 2 + half
                    if r >= 2:
                        K.te.wait(p_evac.at(evac_b + (r - 2) + 1))
                    pb = pst[r % 2]
                    for j in range(4):
                        c = half * 4 + j
                        last = teng.transpose(
                            pb[:, j * 128:(j + 1) * 128],
                            xl[:, t * D + c * 128: t * D + (c + 1) * 128],
                            K.ident[:])
                    tr_toks.append(K.te.inc(last))
        if upto > -3:
            K.T(A_tr)

        def A_evac(v, b=b, p_evac=p_evac):
            for t in range(NT_B):
                for half in range(2):
                    r = t * 2 + half
                    K.ve.wait(tr_toks[r])
                    pb = pst[r % 2]
                    for j in range(4):
                        c = half * 4 + j
                        inst = v.tensor_copy(
                            xfm[:, c * S + t * 128: c * S + (t + 1) * 128],
                            pb[:, j * 128:(j + 1) * 128])
                        if j < 3:
                            K.ve.inc(inst)
                    K.ve.inc(inst, psem=p_evac)
        if upto > -2:
            K.V(A_evac)

        if upto <= -2:
            def dumpB_v(v, b=b):
                K.ve.wait(K.se.tok())
                t = K.ve.inc(v.tensor_copy(yt[:, 0:D], xl[:, 0:D]))
                K.ve.wait(t)
            K.V(dumpB_v)

            def dumpB(q, b=b):
                K.sq.wait(K.ve.tok())
                t = K.dsem().inc(q.dma_start(out=x_out[b * S:b * S + 128, :],
                                             in_=yt[:, 0:D]), step=16)
                K.sq.wait(t)
            K.Q(dumpB)
            K.barrier()
            continue

        if upto <= -1:
            def dumpA(q, b=b):
                K.po.wait(K.se.tok(), K.ve.tok())
                t = K.dsem("p").inc(q.dma_start(
                    out=x_out[b * S:(b + 1) * S, :].rearrange(
                        "(t p) d -> p t d", p=128),
                    in_=xl[:].rearrange("p (t d) -> p t d", d=D)), step=16)
                K.po.wait(t)
            K.G(dumpA)
            K.barrier()
            continue

        def M_v1(v, b=b, p_cons=p_cons, cons_b=cons_b):
            K.ve.wait(K.se.tok())
            K.ve.wait(p_cons.at(cons_b + NT_B))
            t1 = K.ve.inc(v.reduce_max(mcol[:], nrm[:], axis=AX.X))
            K.ve.wait(t1)
        K.V(M_v1)

        dmr = K.dsem("p")

        def M_dma(g, b=b, dmr=dmr):
            K.po.wait(K.ve.tok())
            t = dmr.inc(g.dma_start(out=mrow[:], in_=mcol[:]), step=16)
            K.po.wait(t)
        K.G(M_dma)

        def M_v2(v, b=b, dmr=dmr):
            K.ve.wait(dmr.at(dmr.count))
            t1 = K.ve.inc(v.reduce_max(mrow[:, 0:1], mrow[:], axis=AX.X))
            K.ve.wait(t1)
        K.V(M_v2)

        def M_bc(teng, b=b):
            K.te.wait(K.ve.tok())
            t = K.te.inc(teng.matmul(psM[:, 0:1], K.ones_f32[:, 0:128],
                                     mrow[0:1, 0:1], start=True, stop=True))
            K.te.wait(t)
        K.T(M_bc)

        def M_v3(v, b=b):
            K.ve.wait(K.te.tok())
            t1 = K.ve.inc(v.tensor_copy(mcol[:], psM[:, 0:1]))
            K.ve.wait(t1)
            t2 = K.ve.inc(v.tensor_scalar(cneg[:], nrm[:], mcol[:, 0:1], -1.0,
                                          op0=ALU.mult, op1=ALU.mult))
            K.ve.wait(t2)
        K.V(M_v3)
        cneg_tok = K.ve.tok()
        if upto <= 0:
            def dump(q, b=b):
                K.po.wait(cneg_tok, K.se.tok())
                t = K.dsem("p").inc(q.dma_start(
                    out=x_out[b * S:(b + 1) * S, :].rearrange(
                        "(t p) d -> p t d", p=128),
                    in_=xl[:].rearrange("p (t d) -> p t d", d=D)), step=16)
                K.po.wait(t)
            K.G(dump)
            K.barrier()
            continue

        p_exp = K.psem()
        exp_b = p_exp.count
        mmB = {}

        def B_mm(teng, b=b, mmB=mmB, p_exp=p_exp, exp_b=exp_b):
            K.te.wait(cneg_tok)
            for j in range(NT_B):
                for sv in range(4):
                    r = j * 4 + sv
                    if r >= 2:
                        K.te.wait(p_exp.at(exp_b + r - 1))
                    pb = psS[r % 2]
                    for c in range(DC):
                        last = teng.matmul(
                            pb[:],
                            xfm[:, c * S + j * 128: c * S + (j + 1) * 128],
                            xfm[:, c * S + sv * 512: c * S + (sv + 1) * 512],
                            start=(c == 0), stop=(c == DC - 1))
                    mmB[(j, sv)] = K.te.inc(last)
        K.T(B_mm)

        def B_exp(s, b=b, mmB=mmB, p_exp=p_exp):
            for j in range(NT_B):
                for sv in range(4):
                    r = j * 4 + sv
                    K.se.wait(mmB[(j, sv)])
                    K.se.wait(p_exp.at(p_exp.count))
                    K.se.wait(K.se.tok())
                    inst = s.activation(esc[:], psS[r % 2][:], AF.Exp,
                                        bias=cneg[:, j:j + 1], scale=1.0,
                                        accum_out=zpart[:, sv:sv + 1])
                    p_exp.inc(inst)
                K.se.wait(p_exp.at(p_exp.count))
                t1 = K.se.inc(s.copy(zz[:, j:j + 1], zpart[:, 0:1]))
                for sv in range(1, 4):
                    K.se.wait(t1)
                    t1 = K.se.inc(s.add(zz[:, j:j + 1], zpart[:, sv:sv + 1],
                                        zz[:, j:j + 1]))
        K.Sx(B_exp)

        def B_xp(v, b=b):
            K.ve.wait(K.se.tok())
            t1 = K.ve.inc(v.reciprocal(zz[:], zz[:]))
            K.ve.wait(t1)
            t = None
            for j in range(NT_B):
                t = K.ve.inc(v.tensor_scalar(
                    xl[:, j * D:(j + 1) * D], xl[:, j * D:(j + 1) * D],
                    zz[:, j:j + 1], None, op0=ALU.mult))
            K.ve.wait(t)
        K.V(B_xp)
        xp_tok = K.ve.tok()
        if upto <= 1:
            def dump(q, b=b):
                K.po.wait(xp_tok)
                t = K.dsem("p").inc(q.dma_start(
                    out=x_out[b * S:(b + 1) * S, :].rearrange(
                        "(t p) d -> p t d", p=128),
                    in_=xl[:].rearrange("p (t d) -> p t d", d=D)), step=16)
                K.po.wait(t)
            K.G(dump)
            K.barrier()
            continue

        p_epi = K.psem()
        epi_b = p_epi.count
        p_var = K.psem()
        var_b = p_var.count
        p_rstd = K.psem()
        rstd_b = p_rstd.count
        p_app = K.psem()
        app_b = p_app.count
        d2 = [K.dsem(), K.dsem()]
        d2_b = [d.count for d in d2]
        dot = [K.dsem("p"), K.dsem("p")]
        dot_b = [d.count for d in dot]
        for st_i in range(4):
            lc0 = st_i * 512
            p_exp2 = K.psem()
            exp2_b = p_exp2.count
            mmC = {}

            def C_mm(teng, b=b, lc0=lc0, mmC=mmC, p_exp2=p_exp2,
                     exp2_b=exp2_b):
                K.te.wait(xp_tok)
                for j in range(NT_B):
                    if j >= 2:
                        K.te.wait(p_exp2.at(exp2_b + j - 1))
                    pb = psS[j % 2]
                    for c in range(DC):
                        last = teng.matmul(
                            pb[:],
                            xfm[:, c * S + j * 128: c * S + (j + 1) * 128],
                            xfm[:, c * S + lc0: c * S + lc0 + 512],
                            start=(c == 0), stop=(c == DC - 1))
                    mmC[j] = K.te.inc(last)
            K.T(C_mm)

            def C_exp(s, b=b, mmC=mmC, p_exp2=p_exp2):
                for j in range(NT_B):
                    K.se.wait(mmC[j])
                    inst = s.activation(eall[:, j * 512:(j + 1) * 512],
                                        psS[j % 2][:], AF.Exp,
                                        bias=cneg[:, j:j + 1], scale=1.0)
                    p_exp2.inc(inst)
            K.Sx(C_exp)
            eall_tok = (p_exp2.h, p_exp2.count)

            pv_toks = []

            def C_pv(teng, b=b, st_i=st_i, pv_toks=pv_toks,
                     eall_tok=eall_tok, p_epi=p_epi, epi_b=epi_b):
                K.te.wait(eall_tok)
                for i in range(4):
                    g = st_i * 4 + i
                    if g >= 2:
                        K.te.wait(p_epi.at(epi_b + g - 1))
                    pv = psPV[g % 2]
                    for j in range(NT_B):
                        for h in range(2):
                            last = teng.matmul(
                                pv[:, h * 512:(h + 1) * 512],
                                eall[:, j * 512 + i * 128: j * 512 + (i + 1) * 128],
                                xl[:, j * D + h * 512: j * D + (h + 1) * 512],
                                start=(j == 0), stop=(j == NT_B - 1))
                    pv_toks.append(K.te.inc(last))
            K.T(C_pv)

            ld2 = []

            def C_ld2(q, b=b, st_i=st_i, lc0=lc0, ld2=ld2, p_app=p_app,
                      app_b=app_b):
                for i in range(4):
                    g = st_i * 4 + i
                    if g >= 2:
                        K.sq.wait(p_app.at(app_b + (g - 2) + 1))
                    r0 = b * S + lc0 + i * 128
                    ld2.append(d2[g % 2].inc(q.dma_start(
                        out=x2[g % 2][:], in_=x_in[r0:r0 + 128, :]),
                        step=16))
            K.Q(C_ld2)

            def C_epi_act(s, b=b, st_i=st_i, pv_toks=pv_toks, p_epi=p_epi,
                          p_var=p_var, var_b=var_b, p_rstd=p_rstd,
                          p_app=p_app, app_b=app_b):
                for i in range(4):
                    g = st_i * 4 + i
                    K.se.wait(pv_toks[i])
                    if g >= 2:
                        K.se.wait(p_app.at(app_b + (g - 2) + 1))
                    st8 = stt[:, (g % 2) * 8:(g % 2) * 8 + 8]
                    inst = s.activation(
                        yt[:, (g % 2) * D:(g % 2) * D + D],
                        psPV[g % 2][:], AF.Copy, scale=1.0,
                        accum_out=st8[:, 0:1])
                    K.se.inc(inst, psem=p_epi)
                    K.se.wait(p_var.at(var_b + g + 1))
                    t6 = K.se.inc(s.activation(st8[:, 4:5], st8[:, 3:4],
                                               AF.Ln, bias=K.eps_ap,
                                               scale=1.0))
                    K.se.wait(t6)
                    inst2 = s.activation(st8[:, 4:5], st8[:, 4:5], AF.Exp,
                                         bias=K.zero_ap, scale=-0.5)
                    K.se.inc(inst2, psem=p_rstd)
            K.Sx(C_epi_act)

            def C_epi_dve(v, b=b, st_i=st_i, ld2=ld2, p_epi=p_epi,
                          p_var=p_var, p_rstd=p_rstd, rstd_b=rstd_b,
                          p_app=p_app, dot=dot, dot_b=dot_b, d2_b=d2_b,
                          epi_b=epi_b):
                for i in range(4):
                    g = st_i * 4 + i
                    K.ve.wait(p_epi.at(epi_b + g + 1))
                    K.ve.wait(K.ve.tok())
                    st8 = stt[:, (g % 2) * 8:(g % 2) * 8 + 8]
                    ytile = yt[:, (g % 2) * D:(g % 2) * D + D]
                    t2 = K.ve.inc(v.tensor_scalar(st8[:, 1:2], st8[:, 0:1],
                                                  1.0 / D, None,
                                                  op0=ALU.mult))
                    t3 = K.ve.inc(v.scalar_tensor_tensor(
                        K.sqscr[:, 0:D], ytile, 1.0, ytile, op0=ALU.mult,
                        op1=ALU.mult, accum_out=st8[:, 2:3]))
                    K.ve.wait(t2, t3)
                    t4 = K.ve.inc(v.tensor_scalar(st8[:, 5:6], st8[:, 1:2],
                                                  st8[:, 1:2], None,
                                                  op0=ALU.mult))
                    K.ve.wait(t4)
                    inst = v.tensor_scalar(st8[:, 3:4], st8[:, 2:3], 1.0 / D,
                                           st8[:, 5:6], op0=ALU.mult,
                                           op1=ALU.subtract)
                    K.ve.inc(inst, psem=p_var)
                    K.ve.wait(p_rstd.at(rstd_b + g + 1))
                    ntile = nt_[:, (g % 2) * D:(g % 2) * D + D]
                    t5 = K.ve.inc(v.tensor_scalar(ntile, ytile, st8[:, 1:2],
                                                  st8[:, 4:5],
                                                  op0=ALU.subtract,
                                                  op1=ALU.mult))
                    K.ve.wait(t5)
                    K.ve.wait(ld2[i])
                    if g >= 2:
                        K.ve.wait(dot[g % 2].at(dot_b[g % 2]
                                                + 16 * (g // 2)))
                    inst = v.tensor_add(ot[g % 2][:], ntile, x2[g % 2][:])
                    add_toks.append(K.ve.inc(inst, psem=p_app))
            add_toks = []
            K.V(C_epi_dve)

            def C_st(q, b=b, st_i=st_i, add_toks=add_toks, dot=dot):
                for i in range(4):
                    gg = st_i * 4 + i
                    K.po.wait(add_toks[i])
                    r0 = b * S + st_i * 512 + i * 128
                    t = dot[gg % 2].inc(
                        q.dma_start(out=x_out[r0:r0 + 128, :],
                                    in_=ot[gg % 2][:]), step=16)
                K.po.wait(t)
            K.G(C_st)
        K.barrier()
    K.barrier()



def emit_weight_prep(K, es_outer, jobs):
    nc = K.nc
    es = ExitStack()
    maxc = max(j[0].shape[1] for j in jobs)
    wld = [es.enter_context(nc.sbuf_tensor(f"wp_ld{i}", [128, maxc], F32))
           for i in range(2)]
    wldc = es.enter_context(nc.sbuf_tensor("wp_ldc", [128, maxc], F32))
    wst = [es.enter_context(nc.sbuf_tensor(f"wp_st{i}", [128, maxc], BF16))
           for i in range(2)]
    dld = [K.dsem(), K.dsem()]
    dst_ = [K.dsem("p"), K.dsem("p")]
    dld_b = [d.count for d in dld]
    dst_b = [d.count for d in dst_]
    p_cast = K.psem()
    cast_b = p_cast.count
    chunks = []
    for (sap, dap) in jobs:
        R, C = sap.shape
        assert R % 128 == 0
        for r0 in range(0, R, 128):
            chunks.append((sap[r0:r0 + 128, :], dap[r0:r0 + 128, :], C))
    n = len(chunks)

    def W_ld(q):
        for i, (sap, dap, C) in enumerate(chunks):
            if i >= 2:
                K.sq.wait(p_cast.at(cast_b + (i - 2) + 1))
            if sap.dtype == I32:
                dld[i % 2].inc(q.dma_start(
                    out=wld[i % 2][:, 0:C].bitcast(I32), in_=sap), step=16)
            else:
                dld[i % 2].inc(q.dma_start(out=wld[i % 2][:, 0:C], in_=sap),
                               step=16)
    K.Q(W_ld)

    def W_cast(v):
        for i, (sap, dap, C) in enumerate(chunks):
            K.ve.wait(dld[i % 2].at(dld_b[i % 2] + 16 * (i // 2 + 1)))
            if i >= 2:
                K.ve.wait(dst_[i % 2].at(dst_b[i % 2] + 16 * (i // 2)))
            src_ap = wld[i % 2][:, 0:C]
            if sap.dtype == I32:
                K.ve.wait(K.ve.tok())
                K.ve.wait(p_cast.at(p_cast.count))
                t = K.ve.inc(v.tensor_copy(wldc[:, 0:C],
                                           src_ap.bitcast(I32)))
                K.ve.wait(t)
                src_ap = wldc[:, 0:C]
            inst = v.tensor_copy(wst[i % 2][:, 0:C], src_ap)
            K.ve.inc(inst, psem=p_cast)
    K.V(W_cast)

    def W_st(g):
        for i, (sap, dap, C) in enumerate(chunks):
            K.po.wait(p_cast.at(cast_b + i + 1))
            dst_[i % 2].inc(g.dma_start(out=dap, in_=wst[i % 2][:, 0:C]),
                            step=16)
    K.G(W_st)
    K.barrier()
    es.close()



def emit_l0_ffn(K, es, x_in, x_out, W, kind="l0"):
    nc = K.nc
    ST = 512
    NST = TOK // ST
    HC = 32 if kind == "l0" else 24
    with ExitStack() as es2:
        xt = [es2.enter_context(nc.sbuf_tensor(f"ff_xt{i}", [128, D], F32))
              for i in range(2)]
        x2 = [es2.enter_context(nc.sbuf_tensor(f"ff_x2{i}", [128, D], F32))
              for i in range(2)]
        xl = es2.enter_context(nc.sbuf_tensor("ff_xl", [128, 4 * D], BF16))
        xfm = es2.enter_context(nc.sbuf_tensor("ff_xfm", [128, DC * ST],
                                               BF16))
        hb1 = es2.enter_context(nc.sbuf_tensor("ff_hb1", [128, HC * ST],
                                               BF16))
        hb2 = es2.enter_context(nc.sbuf_tensor("ff_hb2", [128, HC * ST],
                                               BF16))
        wcol = [es2.enter_context(nc.sbuf_tensor(f"ff_w{i}",
                                                 [128, HC * 128], BF16))
                for i in range(2)]
        scol = es2.enter_context(nc.sbuf_tensor("ff_scol", [128, 4 * HC],
                                                F32))
        s3b = es2.enter_context(nc.sbuf_tensor("ff_s3b", [128, D], F32))
        s3r = es2.enter_context(nc.sbuf_tensor("ff_s3r", [1, D], F32))
        stt = es2.enter_context(nc.sbuf_tensor("ff_st", [128, 16], F32))
        yt = es2.enter_context(nc.sbuf_tensor("ff_yt", [128, 2 * D], F32))
        nt_ = es2.enter_context(nc.sbuf_tensor("ff_nt", [128, 2 * D], F32))
        ot = [es2.enter_context(nc.sbuf_tensor(f"ff_ot{i}", [128, D], F32))
              for i in range(2)]
        psA = [es2.enter_context(nc.psum_tensor(f"ff_psA{i}", [128, 512],
                                                F32)) for i in range(2)]
        psO = [es2.enter_context(nc.psum_tensor(f"ff_psO{i}", [128, D],
                                                F32)) for i in range(2)]
        pst = [es2.enter_context(nc.psum_tensor(f"ff_pst{i}", [128, 512],
                                                BF16)) for i in range(2)]

        if kind == "l0":
            def S_ld(q):
                ds = K.dsem()
                for i, nm in enumerate(["s1", "s2", "sh1", "sh2"]):
                    ds.inc(q.dma_start(
                        out=scol[:, i * HC:(i + 1) * HC],
                        in_=W[nm]), step=16)
                t = ds.inc(q.dma_start(
                    out=s3r[:], in_=W["s3"].rearrange("(o d) -> o d", o=1)),
                    step=16)
                K.sq.wait(t)
            K.Q(S_ld)
            K.barrier()

            def S3_mm(te):
                K.te.wait(K.sq.tok())
                t = K.te.inc(te.matmul(psO[0][:, 0:512],
                                       K.ones_f32[:, 0:128],
                                       s3r[:, 0:512], start=True, stop=True))
                t = K.te.inc(te.matmul(psO[1][:, 0:512],
                                       K.ones_f32[:, 0:128],
                                       s3r[:, 512:1024], start=True,
                                       stop=True))
                K.te.wait(t)
            K.T(S3_mm)

            def S3_ev(v):
                K.ve.wait(K.te.tok())
                K.ve.inc(v.tensor_copy(s3b[:, 0:512], psO[0][:, 0:512]))
                t = K.ve.inc(v.tensor_copy(s3b[:, 512:1024],
                                           psO[1][:, 0:512]))
                K.ve.wait(t)
            K.V(S3_ev)
            K.barrier()
        else:
            def S_z(v):
                t = K.ve.inc(v.memset(scol[:], 1.0 / 127.0))
                K.ve.wait(t)
            K.V(S_z)
            K.barrier()

        for sp in range(NST):
            tok0 = sp * ST
            p_cons = K.psem()
            cons_b = p_cons.count
            p_evac = K.psem()
            evac_b = p_evac.count
            dx = [K.dsem(), K.dsem()]
            ld = []

            def F_ld(q, sp=sp, tok0=tok0, ld=ld, p_cons=p_cons,
                     cons_b=cons_b, dx=dx):
                for t in range(4):
                    if t >= 2:
                        K.sq.wait(p_cons.at(cons_b + (t - 2) + 1))
                    r0 = tok0 + t * 128
                    ld.append(dx[t % 2].inc(
                        q.dma_start(out=xt[t % 2][:],
                                    in_=x_in[r0:r0 + 128, :]), step=16))
            K.Q(F_ld)

            var_toks = []

            def F_vec(v, sp=sp, ld=ld):
                for t in range(4):
                    K.ve.wait(ld[t])
                    st8 = stt[:, (t % 2) * 8:(t % 2) * 8 + 8]
                    var_toks.append(ln_stats_v(K, v, xt[t % 2][:], st8))
            K.V(F_vec)

            ap_toks = []

            def F_sc(s, sp=sp, p_cons=p_cons):
                for t in range(4):
                    K.se.wait(var_toks[t])
                    st8 = stt[:, (t % 2) * 8:(t % 2) * 8 + 8]
                    tA, _ = ln_apply_s(K, s, xt[t % 2][:],
                                       xl[:, t * D:(t + 1) * D], st8,
                                       extra_psem=p_cons)
                    ap_toks.append(tA)
            K.Sx(F_sc)

            tr_toks = []

            def F_tr(teng, sp=sp, p_evac=p_evac, evac_b=evac_b):
                for t in range(4):
                    K.te.wait(ap_toks[t])
                    for half in range(2):
                        r = t * 2 + half
                        if r >= 2:
                            K.te.wait(p_evac.at(evac_b + (r - 2) + 1))
                        pb = pst[r % 2]
                        for j in range(4):
                            c = half * 4 + j
                            last = teng.transpose(
                                pb[:, j * 128:(j + 1) * 128],
                                xl[:, t * D + c * 128:t * D + (c + 1) * 128],
                                K.ident[:])
                        tr_toks.append(K.te.inc(last))
            K.T(F_tr)

            def F_evac(v, sp=sp, p_evac=p_evac):
                for t in range(4):
                    for half in range(2):
                        r = t * 2 + half
                        K.ve.wait(tr_toks[r])
                        pb = pst[r % 2]
                        for j in range(4):
                            c = half * 4 + j
                            inst = v.tensor_copy(
                                xfm[:, c * ST + t * 128:c * ST + (t + 1) * 128],
                                pb[:, j * 128:(j + 1) * 128])
                            if j < 3:
                                K.ve.inc(inst)
                        K.ve.inc(inst, psem=p_evac)
            K.V(F_evac)
            xfm_tok = (p_evac.h, evac_b + 8)

            def fm_stage(blob, KC, MC, src_fm, dst, act, scale_off,
                         entry_tok, mul_gate=None):
                p_wld = [K.dsem(), K.dsem()]
                wld_b = [d.count for d in p_wld]
                p_mm = K.psem()
                mm_b = p_mm.count
                p_ev = K.psem()
                ev_b = p_ev.count

                def W_ld(q):
                    for mc in range(MC):
                        if mc < 2:
                            K.sq.wait(entry_tok)
                        else:
                            K.sq.wait(p_mm.at(mm_b + (mc - 2) + 1))
                        r0 = mc * KC * 128
                        p_wld[mc % 2].inc(q.dma_start(
                            out=wcol[mc % 2][:, 0:KC * 128].rearrange(
                                "p (kc m) -> p kc m", m=128),
                            in_=blob[r0:r0 + KC * 128, :].rearrange(
                                "(kc p) m -> p kc m", p=128)), step=16)
                K.Q(W_ld)

                mm_toks = []

                def W_mm(teng):
                    K.te.wait(entry_tok)
                    for mc in range(MC):
                        K.te.wait(p_wld[mc % 2].at(
                            wld_b[mc % 2] + 16 * (mc // 2 + 1)))
                        if mc >= 2:
                            K.te.wait(p_ev.at(ev_b + (mc - 2) + 1))
                        pa = psA[mc % 2]
                        for kc in range(KC):
                            last = teng.matmul(
                                pa[:],
                                wcol[mc % 2][:, kc * 128:(kc + 1) * 128],
                                src_fm[:, kc * ST:(kc + 1) * ST],
                                start=(kc == 0), stop=(kc == KC - 1))
                        mm_toks.append(K.te.inc(last, psem=p_mm))
                K.T(W_mm)

                def W_ev(s):
                    for mc in range(MC):
                        K.se.wait(mm_toks[mc])
                        sc_ap = scol[:, scale_off + mc:scale_off + mc + 1]
                        inst = s.activation(dst[:, mc * ST:(mc + 1) * ST],
                                            psA[mc % 2][:], act,
                                            bias=K.zero_ap, scale=sc_ap)
                        K.se.inc(inst, psem=p_ev)
                K.Sx(W_ev)

                mul_tok = None
                if mul_gate is not None:
                    mg_p, mg_b = mul_gate

                    def W_mul(v):
                        for mc in range(MC):
                            K.ve.wait(p_ev.at(ev_b + mc + 1))
                            K.ve.wait(mg_p.at(mg_b + mc + 1))
                            K.ve.inc(v.tensor_mul(
                                dst[:, mc * ST:(mc + 1) * ST],
                                dst[:, mc * ST:(mc + 1) * ST],
                                hb2[:, mc * ST:(mc + 1) * ST]))
                    K.V(W_mul)
                    mul_tok = K.ve.tok()
                return p_ev, ev_b, mul_tok

            if kind == "l0":
                pe2, eb2, _ = fm_stage(W["w2b"], DC, HC, xfm, hb2,
                                       AF.Identity, HC, xfm_tok)
                pe1, eb1, mul_tok = fm_stage(W["w1b"], DC, HC, xfm, hb1,
                                             AF.Silu, 0, (pe2.h, eb2 + HC),
                                             mul_gate=(pe2, eb2))
                peh1, ebh1, _ = fm_stage(W["h1b"], HC, HC, hb1, hb2,
                                         AF.Gelu, 2 * HC, mul_tok)
                peh2, ebh2, _ = fm_stage(W["h2b"], HC, HC, hb2, hb1,
                                         AF.Gelu, 3 * HC,
                                         (peh1.h, ebh1 + HC))
                h2_done_tok = (peh2.h, ebh2 + HC)
                w3_KC = HC
                w3_blob = W["w3b"]
            else:
                pe2, eb2, _ = fm_stage(W["wub"], DC, HC, xfm, hb2,
                                       AF.Identity, 0, xfm_tok)
                pe1, eb1, mul_tok = fm_stage(W["wgb"], DC, HC, xfm, hb1,
                                             AF.Silu, 0, (pe2.h, eb2 + HC),
                                             mul_gate=(pe2, eb2))
                h2_done_tok = mul_tok
                w3_KC = HC
                w3_blob = W["wdb"]

            p_mm3 = K.psem()
            mm3_b = p_mm3.count
            p_epi = K.psem()
            epi_b = p_epi.count
            p_var = K.psem()
            var_b = p_var.count
            p_rstd = K.psem()
            rstd_b = p_rstd.count
            p_app = K.psem()
            app_b = p_app.count
            w3ld = [K.dsem(), K.dsem()]
            w3ld_b = [d.count for d in w3ld]
            d2 = [K.dsem(), K.dsem()]
            dot = [K.dsem("p"), K.dsem("p")]
            dot_b = [d.count for d in dot]
            h2_done = h2_done_tok

            ld2 = []

            def F_ld2(gs):
                def f(q, gs=gs, sp=sp, tok0=tok0, ld2=ld2, p_app=p_app,
                      app_b=app_b, d2=d2):
                    for g in gs:
                        if g >= 2:
                            K.sq.wait(p_app.at(app_b + (g - 2) + 1))
                        r0 = tok0 + g * 128
                        ld2.append(d2[g % 2].inc(q.dma_start(
                            out=x2[g % 2][:], in_=x_in[r0:r0 + 128, :]),
                            step=16))
                return f

            def W3_ld(pair):
                def f(q, pair=pair, sp=sp):
                    for mc in range(8):
                        i = pair * 8 + mc
                        if i < 2:
                            K.sq.wait(h2_done)
                        else:
                            K.sq.wait(p_mm3.at(mm3_b + (i - 2) + 1))
                        r0 = mc * w3_KC * 128
                        w3ld[i % 2].inc(q.dma_start(
                            out=wcol[i % 2][:, 0:w3_KC * 128].rearrange(
                                "p (kc m) -> p kc m", m=128),
                            in_=w3_blob[r0:r0 + w3_KC * 128, :].rearrange(
                                "(kc p) m -> p kc m", p=128)), step=16)
                return f
            K.Q(F_ld2([0, 1]))
            K.Q(W3_ld(0))
            K.Q(F_ld2([2, 3]))
            K.Q(W3_ld(1))

            def W3_mm(teng, sp=sp):
                K.te.wait(h2_done)
                for pair in range(2):
                    if pair >= 1:
                        K.te.wait(p_epi.at(epi_b + 2 * pair))
                    for mc in range(8):
                        i = pair * 8 + mc
                        K.te.wait(w3ld[i % 2].at(
                            w3ld_b[i % 2] + 16 * (i // 2 + 1)))
                        last2 = None
                        for tt in range(2):
                            t_g = pair * 2 + tt
                            for kc in range(w3_KC):
                                last2 = teng.matmul(
                                    psO[tt][:, mc * 128:(mc + 1) * 128],
                                    hb1[:, kc * ST + t_g * 128:
                                        kc * ST + (t_g + 1) * 128],
                                    wcol[i % 2][:, kc * 128:(kc + 1) * 128],
                                    start=(kc == 0), stop=(kc == w3_KC - 1))
                        K.te.inc(last2, psem=p_mm3)
            K.T(W3_mm)

            def E_dve(v, sp=sp, ld2=ld2):
                for g in range(4):
                    pair, tt = g // 2, g % 2
                    K.ve.wait(p_mm3.at(mm3_b + pair * 8 + 8))
                    K.ve.wait(K.ve.tok())
                    st8 = stt[:, (g % 2) * 8:(g % 2) * 8 + 8]
                    ytile = yt[:, (g % 2) * D:(g % 2) * D + D]
                    if kind == "l0":
                        inst = v.scalar_tensor_tensor(
                            ytile, psO[tt][:], 1.0, s3b[:], op0=ALU.mult,
                            op1=ALU.mult, accum_out=st8[:, 0:1])
                    else:
                        inst = v.tensor_scalar(
                            ytile, psO[tt][:], 1.0 / 127.0, 0.0,
                            op0=ALU.mult, op1=ALU.add,
                            accum_out=st8[:, 0:1])
                    K.ve.inc(inst, psem=p_epi)
                    K.ve.wait(p_epi.at(p_epi.count))
                    t2 = K.ve.inc(v.tensor_scalar(st8[:, 1:2], st8[:, 0:1],
                                                  1.0 / D, None,
                                                  op0=ALU.mult))
                    t3 = K.ve.inc(v.scalar_tensor_tensor(
                        K.sqscr[:, 0:D], ytile, 1.0, ytile, op0=ALU.mult,
                        op1=ALU.mult, accum_out=st8[:, 2:3]))
                    K.ve.wait(t2, t3)
                    t4 = K.ve.inc(v.tensor_scalar(st8[:, 5:6], st8[:, 1:2],
                                                  st8[:, 1:2], None,
                                                  op0=ALU.mult))
                    K.ve.wait(t4)
                    inst = v.tensor_scalar(st8[:, 3:4], st8[:, 2:3], 1.0 / D,
                                           st8[:, 5:6], op0=ALU.mult,
                                           op1=ALU.subtract)
                    K.ve.inc(inst, psem=p_var)
                    K.ve.wait(p_rstd.at(rstd_b + g + 1))
                    ntile = nt_[:, (g % 2) * D:(g % 2) * D + D]
                    t5 = K.ve.inc(v.tensor_scalar(ntile, ytile, st8[:, 1:2],
                                                  st8[:, 4:5],
                                                  op0=ALU.subtract,
                                                  op1=ALU.mult))
                    K.ve.wait(t5, ld2[g])
                    if g >= 2:
                        K.ve.wait(dot[g % 2].at(dot_b[g % 2]
                                                + 16 * (g // 2)))
                    inst = v.tensor_add(ot[g % 2][:], ntile, x2[g % 2][:])
                    K.ve.inc(inst, psem=p_app)
            K.V(E_dve)

            def E_act(s, sp=sp):
                for g in range(4):
                    st8 = stt[:, (g % 2) * 8:(g % 2) * 8 + 8]
                    K.se.wait(p_var.at(var_b + g + 1))
                    t6 = K.se.inc(s.activation(st8[:, 4:5], st8[:, 3:4],
                                               AF.Ln, bias=K.eps_ap,
                                               scale=1.0))
                    K.se.wait(t6)
                    inst = s.activation(st8[:, 4:5], st8[:, 4:5], AF.Exp,
                                        bias=K.zero_ap, scale=-0.5)
                    K.se.inc(inst, psem=p_rstd)
            K.Sx(E_act)

            def F_st(q, sp=sp, tok0=tok0, dot=dot, p_app=p_app,
                     app_b=app_b):
                for g in range(4):
                    K.po.wait(p_app.at(app_b + g + 1))
                    r0 = tok0 + g * 128
                    t = dot[g % 2].inc(
                        q.dma_start(out=x_out[r0:r0 + 128, :],
                                    in_=ot[g % 2][:]), step=16)
                K.po.wait(t)
            K.G(F_st)
            K.barrier()



def emit_l3_attn(K, es, x_in, x_out, W):
    nc = K.nc
    SB = S
    H, DH = 8, 64
    RS2 = float(2.0 ** -0.5)
    with ExitStack() as es2:
        xt = [es2.enter_context(nc.sbuf_tensor(f"a3_xt{i}", [128, D], F32))
              for i in range(2)]
        x2 = xt
        xl = es2.enter_context(nc.sbuf_tensor("a3_xl", [128, NT_B * D],
                                              BF16))
        xfm = es2.enter_context(nc.sbuf_tensor("a3_xfm", [128, DC * SB],
                                               BF16))
        qkfm = es2.enter_context(nc.sbuf_tensor("a3_qkfm", [128, DC * SB],
                                                BF16))
        ofm = xfm
        vtm = es2.enter_context(nc.sbuf_tensor("a3_vtm", [128, NT_B * 520],
                                               BF16))
        qhat = es2.enter_context(nc.sbuf_tensor("a3_qhat", [65, SB], BF16))
        khat = es2.enter_context(nc.sbuf_tensor("a3_khat", [65, SB], BF16))

        wcol = [es2.enter_context(nc.sbuf_tensor(f"a3_w{i}", [128, 2048],
                                                 BF16)) for i in range(2)]

        biasb = es2.enter_context(nc.sbuf_tensor("a3_bias", [128, 32], F32))
        brow = es2.enter_context(nc.sbuf_tensor("a3_brow", [1, 1024], BF16))
        borow = es2.enter_context(nc.sbuf_tensor("a3_bo", [1, 1024], BF16))
        gb = es2.enter_context(nc.sbuf_tensor("a3_gb", [128, D], F32))
        bb = es2.enter_context(nc.sbuf_tensor("a3_bb", [128, D], F32))

        nrm3 = es2.enter_context(nc.sbuf_tensor("a3_nrm3", [65, SB], F32))
        gbr = nrm3[0:1, :]
        mval = es2.enter_context(nc.sbuf_tensor("a3_mval", [1, 1], F32))

        et = [es2.enter_context(nc.sbuf_tensor(f"a3_et{i}", [128, 512],
                                               BF16)) for i in range(2)]
        zr = es2.enter_context(nc.sbuf_tensor("a3_zr", [1, 512], F32))
        obuf = es2.enter_context(nc.sbuf_tensor("a3_obuf", [64, 512], BF16))
        stt = es2.enter_context(nc.sbuf_tensor("a3_st", [128, 16], F32))
        yt = es2.enter_context(nc.sbuf_tensor("a3_yt", [128, 2 * D], F32))
        zt = yt
        sq2 = yt[:].bitcast(BF16)
        ot = [es2.enter_context(nc.sbuf_tensor(f"a3_ot{i}", [128, D], F32))
              for i in range(2)]
        psA = [es2.enter_context(nc.psum_tensor(f"a3_psA{i}", [128, 512],
                                                F32)) for i in range(2)]
        psS = [es2.enter_context(nc.psum_tensor(f"a3_psS{i}", [128, 512],
                                                F32)) for i in range(2)]
        psOA = es2.enter_context(nc.psum_tensor("a3_psOA", [128, 512], F32))
        pst = [es2.enter_context(nc.psum_tensor(f"a3_pst{i}", [128, 512],
                                                BF16)) for i in range(2)]

        DS = K.named_psems(es2, "a3d", ["dx0", "dx1", "dqk0", "dqk1",
                                        "dv", "d20", "d21", "dot0", "dot1",
                                        "dwo", "dqh", "bldg", "bld"])
        P = K.named_psems(es2, "a3", ['cons', 'evac', 'qk', 'vt', 'sq', 'sqc', 'sqm', 'exp', 'pvj', 'zr', 'bc', 'oevac', 'oev', 'mm4', 'epi', 'var', 'rstd', 'pool', 'z2', 'var2', 'rstd2', 'app'])
        def B_ld(q):
            ds = DS["bld"]
            ds.inc(q.dma_start(out=biasb[:, 0:8], in_=W["bqk"]), step=16)
            ds.inc(q.dma_start(out=biasb[:, 16:17], in_=W["qsc"]), step=16)
            ds.inc(q.dma_start(out=gbr[0:1, 0:D], in_=W["gam"].rearrange(
                "(o d) -> o d", o=1)), step=16)
            t = ds.inc(q.dma_start(out=gbr[0:1, D:2 * D],
                                   in_=W["bet"].rearrange(
                                       "(o d) -> o d", o=1)), step=16)
            K.sq.wait(t)
        K.Q(B_ld)

        def B_ldg(g):
            ds = DS["bldg"]
            ds.inc(g.dma_start(out=brow[:, 0:512], in_=W["bv"].rearrange(
                "(o d) -> o d", o=1)), step=16)
            t = ds.inc(g.dma_start(out=borow[:], in_=W["bo"].rearrange(
                "(o d) -> o d", o=1)), step=16)
            K.po.wait(t)
        K.G(B_ldg)
        K.barrier()

        def B_prep_v(v):
            for mc in range(8):
                sc = 0.125 if mc < 4 else 1.0
                K.ve.inc(v.tensor_scalar(biasb[:, 8 + mc:9 + mc],
                                         biasb[:, mc:mc + 1], sc, None,
                                         op0=ALU.mult))
            K.ve.wait(K.ve.tok())
        K.V(B_prep_v)

        def B_gb_mm(te):
            K.te.wait(K.ve.tok())
            te.matmul(psA[0][:], K.ones_f32[:, 0:128], gbr[0:1, 0:512],
                      start=True, stop=True)
            te.matmul(psA[1][:], K.ones_f32[:, 0:128], gbr[0:1, 512:1024],
                      start=True, stop=True)
            te.matmul(psS[0][:], K.ones_f32[:, 0:128],
                      gbr[0:1, 1024:1536], start=True, stop=True)
            t = K.te.inc(te.matmul(psS[1][:], K.ones_f32[:, 0:128],
                                   gbr[0:1, 1536:2048], start=True,
                                   stop=True))
            K.te.wait(t)
        K.T(B_gb_mm)

        def B_gb_ev(v):
            K.ve.wait(K.te.tok())
            K.ve.inc(v.tensor_copy(gb[:, 0:512], psA[0][:]))
            K.ve.inc(v.tensor_copy(gb[:, 512:1024], psA[1][:]))
            K.ve.inc(v.tensor_copy(bb[:, 0:512], psS[0][:]))
            t = K.ve.inc(v.tensor_copy(bb[:, 512:1024], psS[1][:]))
            K.ve.wait(t)
        K.V(B_gb_ev)

        def B_khat_row(v):
            t = K.ve.inc(v.memset(khat[64:65, :], -1.0))
            K.ve.wait(t)
        K.V(B_khat_row)

        def B_vones(v):
            inst = v.memset(vtm[:].rearrange(
                "p (t h w) -> p t h w", t=NT_B, h=H)[:, :, :, 64:65], 1.0)
            t = K.ve.inc(inst)
            K.ve.wait(t)
        K.V(B_vones)
        K.barrier()

        for b in range(B_PER_CORE):
            p_cons = P["cons"]
            cons_b = p_cons.count
            p_evac = P["evac"]
            evac_b = p_evac.count
            dx = [DS["dx0"], DS["dx1"]]
            ld = []

            def P1_ld(q, b=b, ld=ld, p_cons=p_cons, cons_b=cons_b, dx=dx):
                for t in range(NT_B):
                    if t >= 2:
                        K.sq.wait(p_cons.at(cons_b + (t - 2) + 1))
                    r0 = b * S + t * 128
                    ld.append(dx[t % 2].inc(
                        q.dma_start(out=xt[t % 2][:],
                                    in_=x_in[r0:r0 + 128, :]), step=16))
            K.Q(P1_ld)

            var_toks = []

            def P1_v(v, b=b, ld=ld):
                for t in range(NT_B):
                    K.ve.wait(ld[t])
                    st8 = stt[:, (t % 2) * 8:(t % 2) * 8 + 8]
                    var_toks.append(ln_stats_v(K, v, xt[t % 2][:], st8))
            K.V(P1_v)

            ap_toks = []

            def P1_s(s, b=b, p_cons=p_cons):
                for t in range(NT_B):
                    K.se.wait(var_toks[t])
                    st8 = stt[:, (t % 2) * 8:(t % 2) * 8 + 8]
                    tA, _ = ln_apply_s(K, s, xt[t % 2][:],
                                       xl[:, t * D:(t + 1) * D], st8,
                                       extra_psem=p_cons)
                    ap_toks.append(tA)
            K.Sx(P1_s)

            tr_toks = []

            def P1_tr(teng, b=b, p_evac=p_evac, evac_b=evac_b):
                for t in range(NT_B):
                    K.te.wait(ap_toks[t])
                    for half in range(2):
                        r = t * 2 + half
                        if r >= 2:
                            K.te.wait(p_evac.at(evac_b + (r - 2) + 1))
                        pb = pst[r % 2]
                        for j in range(4):
                            c = half * 4 + j
                            last = teng.transpose(
                                pb[:, j * 128:(j + 1) * 128],
                                xl[:, t * D + c * 128:t * D + (c + 1) * 128],
                                K.ident[:])
                        tr_toks.append(K.te.inc(last))
            K.T(P1_tr)

            def P1_ev(v, b=b, p_evac=p_evac):
                for t in range(NT_B):
                    for half in range(2):
                        r = t * 2 + half
                        K.ve.wait(tr_toks[r])
                        pb = pst[r % 2]
                        for j in range(4):
                            c = half * 4 + j
                            inst = v.tensor_copy(
                                xfm[:, c * SB + t * 128:c * SB + (t + 1) * 128],
                                pb[:, j * 128:(j + 1) * 128])
                            if j < 3:
                                K.ve.inc(inst)
                        K.ve.inc(inst, psem=p_evac)
            K.V(P1_ev)
            K.barrier()

            dqk = [DS["dqk0"], DS["dqk1"]]
            dqk_b = [d.count for d in dqk]
            p_qk = P["qk"]
            qk_b = p_qk.count

            def P2_qk_ld(q, b=b):
                for mc in range(8):
                    if mc >= 2:
                        K.sq.wait(p_qk.at(qk_b + 4 * (mc - 1)))
                    r0 = mc * DC * 128
                    dqk[mc % 2].inc(q.dma_start(
                        out=wcol[mc % 2][:, 0:DC * 128].rearrange(
                            "p (kc m) -> p kc m", m=128),
                        in_=W["wqkb"][r0:r0 + DC * 128, :].rearrange(
                            "(kc p) m -> p kc m", p=128)), step=16)
            K.Q(P2_qk_ld)

            qkmm = {}

            def P2_qk_mm(teng, b=b):
                for mc in range(8):
                    K.te.wait(dqk[mc % 2].at(dqk_b[mc % 2]
                                             + 16 * (mc // 2 + 1)))
                    for sp4 in range(4):
                        r = mc * 4 + sp4
                        if r >= 2:
                            K.te.wait(p_qk.at(qk_b + r - 1))
                        pa = psA[r % 2]
                        for kc in range(DC):
                            last = teng.matmul(
                                pa[:],
                                wcol[mc % 2][:, kc * 128:(kc + 1) * 128],
                                xfm[:, kc * SB + sp4 * 512:
                                    kc * SB + (sp4 + 1) * 512],
                                start=(kc == 0), stop=(kc == DC - 1))
                        qkmm[(mc, sp4)] = K.te.inc(last)
            K.T(P2_qk_mm)

            def P2_qk_ev(s, b=b):
                for mc in range(8):
                    for sp4 in range(4):
                        r = mc * 4 + sp4
                        K.se.wait(qkmm[(mc, sp4)])
                        sc = 0.125 if mc < 4 else 1.0
                        inst = s.activation(
                            qkfm[:, mc * SB + sp4 * 512:
                                 mc * SB + (sp4 + 1) * 512],
                            psA[r % 2][:], AF.Identity,
                            bias=biasb[:, 8 + mc:9 + mc], scale=sc)
                        K.se.inc(inst, psem=p_qk)
            K.Sx(P2_qk_ev)

            dv = DS["dv"]
            dv_b = dv.count
            p_vt = P["vt"]
            vt_b = p_vt.count

            def P2_v_ld(q, b=b, dv=dv):
                K.sq.wait(p_qk.at(qk_b + 32))
                dv.inc(q.dma_start(
                    out=wcol[0][:].rearrange("p (kc m) -> p kc m", m=512),
                    in_=W["wvb"][0:512, :].rearrange(
                        "(kc p) m -> p kc m", p=128)), step=16)
                dv.inc(q.dma_start(
                    out=wcol[1][:].rearrange("p (kc m) -> p kc m", m=512),
                    in_=W["wvb"][512:1024, :].rearrange(
                        "(kc p) m -> p kc m", p=128)), step=16)
            K.Q(P2_v_ld)

            vmm = {}

            def P2_v_mm(teng, b=b, dv=dv, dv_b=dv_b):
                K.te.wait(dv.at(dv_b + 32))
                for t in range(NT_B):
                    if t >= 2:
                        K.te.wait(p_vt.at(vt_b + t - 1))
                    pa = psS[t % 2]
                    for kc in range(DC):
                        wsrc = wcol[kc // 4][:, (kc % 4) * 512:
                                             (kc % 4 + 1) * 512]
                        teng.matmul(
                            pa[:],
                            xfm[:, kc * SB + t * 128:kc * SB + (t + 1) * 128],
                            wsrc, start=(kc == 0), stop=False)
                    last = teng.matmul(pa[:], K.ones_bf[:, 0:128],
                                       brow[:, 0:512], start=False,
                                       stop=True)
                    vmm[t] = K.te.inc(last)
            K.T(P2_v_mm)

            def P2_v_ev(v, b=b):
                for t in range(NT_B):
                    K.ve.wait(vmm[t])
                    inst = v.tensor_copy(
                        vtm[:, t * 520:(t + 1) * 520].rearrange(
                            "p (h w) -> p h w", w=65)[:, :, 0:64],
                        psS[t % 2][:].rearrange("p (h w) -> p h w", w=64))
                    K.ve.inc(inst, psem=p_vt)
            K.V(P2_v_ev)
            K.barrier()

            prev_head_done = None
            for h in range(H):
                dqh = DS["dqh"]
                dqh_b = dqh.count

                def P3_build(g, b=b, h=h, dqh=dqh,
                             prev_head_done=prev_head_done):
                    if prev_head_done is not None:
                        K.po.wait(prev_head_done)
                    qsrc = qkfm[(h % 2) * 64:(h % 2) * 64 + 64,
                                (h // 2) * SB:(h // 2 + 1) * SB]
                    ksrc = qkfm[(h % 2) * 64:(h % 2) * 64 + 64,
                                (4 + h // 2) * SB:(4 + h // 2 + 1) * SB]
                    dqh.inc(g.dma_start(out=qhat[0:64, :], in_=qsrc),
                            step=16)
                    t = dqh.inc(g.dma_start(out=khat[0:64, :], in_=ksrc),
                                step=16)
                    K.po.wait(t)
                K.G(P3_build)
                bld_tok = (dqh.h, dqh_b + 32)

                p_sq = P["sq"]
                sq_b = p_sq.count
                p_sqm = P["sqm"]
                sqm_b = p_sqm.count

                def P3_sq_s(s, b=b, h=h, bld_tok=bld_tok):
                    K.se.wait(bld_tok)
                    inst = s.square(sq2[0:64, 0:SB], qhat[0:64, :])
                    K.se.inc(inst, psem=p_sq)
                    K.se.wait(p_sqm.at(sqm_b + 4))
                    inst = s.square(sq2[0:64, 0:SB], khat[0:64, :])
                    K.se.inc(inst, psem=p_sq)
                K.Sx(P3_sq_s)

                p_sqc = P["sqc"]
                sqc_b = p_sqc.count

                def P3_sq_mm(teng, b=b, h=h):
                    for which in range(2):
                        K.te.wait(p_sq.at(sq_b + which + 1))
                        for sp4 in range(4):
                            r = which * 4 + sp4
                            if r >= 1:
                                K.te.wait(p_sqc.at(sqc_b + r))
                            inst = teng.matmul(
                                psOA[0:1, 0:512], K.ones_col[0:64, 0:1],
                                sq2[0:64, sp4 * 512:(sp4 + 1) * 512],
                                start=True, stop=True)
                            K.te.inc(inst, psem=p_sqm)
                K.T(P3_sq_mm)

                def P3_sq_ev(v, b=b, h=h):
                    for which in range(2):
                        for sp4 in range(4):
                            r = which * 4 + sp4
                            K.ve.wait(p_sqm.at(sqm_b + r + 1))
                            row = 0 if which == 0 else 32
                            inst = v.tensor_copy(
                                nrm3[row:row + 1,
                                     sp4 * 512:(sp4 + 1) * 512],
                                psOA[0:1, 0:512])
                            K.ve.inc(inst, psem=p_sqc)
                K.V(P3_sq_ev)

                def P3_norm_s(s, b=b, h=h):
                    K.se.wait(p_sqc.at(sqc_b + 8))
                    t = K.se.inc(s.activation(nrm3[0:1, :], nrm3[0:1, :],
                                              AF.Sqrt,
                                              bias=K.zero_ap[0:1, 0:1],
                                              scale=1.0))
                    K.se.wait(t)
                    t = K.se.inc(s.activation(nrm3[32:33, :], nrm3[32:33, :],
                                              AF.Sqrt,
                                              bias=K.zero_ap[0:1, 0:1],
                                              scale=1.0))
                    K.se.wait(t)
                K.Sx(P3_norm_s)

                def P3_m_v(v, b=b, h=h):
                    K.ve.wait(K.se.tok())
                    t = K.ve.inc(v.reduce_max(mval[:], nrm3[32:33, :],
                                              axis=AX.X))
                    K.ve.wait(t)
                    t = K.ve.inc(v.tensor_scalar(nrm3[64:65, :],
                                                 nrm3[0:1, :],
                                                 mval[0:1, 0:1], None,
                                                 op0=ALU.mult))
                    K.ve.wait(t)
                    t = K.ve.inc(v.tensor_copy(qhat[64:65, :],
                                               nrm3[64:65, :]))
                    K.ve.wait(t)
                K.V(P3_m_v)
                hdr_tok = K.ve.tok()

                p_exp = P["exp"]
                exp_b = p_exp.count
                p_pvj = P["pvj"]
                pvj_b = p_pvj.count
                p_zr = P["zr"]
                zr_b = p_zr.count
                p_bc = P["bc"]
                bc_b = p_bc.count
                p_oevac = P["oevac"]
                oevac_b = p_oevac.count
                p_oev = P["oev"]
                oev_b = p_oev.count
                smm = {}

                def P3_te(teng, b=b, h=h, hdr_tok=hdr_tok):
                    K.te.wait(hdr_tok)
                    for sp4 in range(4):
                        for j in range(NT_B):
                            r = sp4 * NT_B + j
                            if r >= 2:
                                K.te.wait(p_exp.at(exp_b + r - 1))
                            inst = teng.matmul(
                                psS[r % 2][:],
                                khat[:, j * 128:(j + 1) * 128],
                                qhat[:, sp4 * 512:(sp4 + 1) * 512],
                                start=True, stop=True)
                            smm[(sp4, j)] = K.te.inc(inst)
                            if j >= 1:
                                jj = j - 1
                                rr = sp4 * NT_B + jj
                                K.te.wait(p_exp.at(exp_b + rr + 1))
                                if jj == 0 and sp4 >= 1:
                                    K.te.wait(p_oev.at(oev_b + sp4))
                                inst = teng.matmul(
                                    psOA[0:65, :],
                                    vtm[:, jj * 520 + h * 65:
                                        jj * 520 + h * 65 + 65],
                                    et[jj % 2][:],
                                    start=(jj == 0), stop=False)
                                K.te.inc(inst, psem=p_pvj)
                        jj = NT_B - 1
                        rr = sp4 * NT_B + jj
                        K.te.wait(p_exp.at(exp_b + rr + 1))
                        inst = teng.matmul(
                            psOA[0:65, :],
                            vtm[:, jj * 520 + h * 65:jj * 520 + h * 65 + 65],
                            et[jj % 2][:], start=False, stop=True)
                        K.te.inc(inst, psem=p_pvj)
                        K.te.wait(p_zr.at(zr_b + sp4 + 1))
                        if sp4 >= 1:
                            K.te.wait(p_oev.at(oev_b + sp4))
                        inst = teng.matmul(psA[0][0:64, 0:512],
                                           K.ones_f32[:, 0:64],
                                           zr[:], start=True, stop=True)
                        K.te.inc(inst, psem=p_bc)
                K.T(P3_te)

                def P3_exp(s, b=b, h=h):
                    for sp4 in range(4):
                        for j in range(NT_B):
                            r = sp4 * NT_B + j
                            K.se.wait(smm[(sp4, j)])
                            if r >= 2:
                                K.se.wait(p_pvj.at(pvj_b + r - 1))
                            inst = s.activation(et[j % 2][:],
                                                psS[r % 2][:], AF.Exp,
                                                bias=K.zero_ap, scale=1.0)
                            K.se.inc(inst, psem=p_exp)
                        K.se.wait(p_pvj.at(pvj_b + (sp4 + 1) * NT_B))
                        if sp4 >= 1:
                            K.se.wait(p_oev.at(oev_b + sp4))
                        inst = s.activation(obuf[:], psOA[0:64, :],
                                            AF.Copy, scale=1.0)
                        K.se.inc(inst, psem=p_oevac)
                K.Sx(P3_exp)

                def P3_odve(v, b=b, h=h):
                    for sp4 in range(4):
                        K.ve.wait(p_pvj.at(pvj_b + (sp4 + 1) * NT_B))
                        inst = v.reciprocal(zr[:], psOA[64:65, 0:512])
                        K.ve.inc(inst, psem=p_zr)
                        K.ve.wait(p_bc.at(bc_b + sp4 + 1))
                        K.ve.wait(p_oevac.at(oevac_b + sp4 + 1))
                        dst = ofm[(h % 2) * 64:(h % 2) * 64 + 64,
                                  (h // 2) * SB + sp4 * 512:
                                  (h // 2) * SB + (sp4 + 1) * 512]
                        inst = v.tensor_tensor(dst, obuf[:],
                                               psA[0][0:64, 0:512],
                                               op=ALU.mult)
                        K.ve.inc(inst, psem=p_oev)
                K.V(P3_odve)
                prev_head_done = (p_oev.h, oev_b + 4)
            K.barrier()

            p_mm4 = P["mm4"]
            mm4_b = p_mm4.count
            p_epi = P["epi"]
            epi_b = p_epi.count
            p_var = P["var"]
            var_b = p_var.count
            p_rstd = P["rstd"]
            rstd_b = p_rstd.count
            p_pool = P["pool"]
            pool_b = p_pool.count
            p_z2 = P["z2"]
            z2_b = p_z2.count
            p_var2 = P["var2"]
            var2_b = p_var2.count
            p_rstd2 = P["rstd2"]
            rstd2_b = p_rstd2.count
            p_app = P["app"]
            app_b = p_app.count
            d2 = [DS["d20"], DS["d21"]]
            dot = [DS["dot0"], DS["dot1"]]
            dot_b = [d.count for d in dot]
            ld2 = []

            dwo = DS["dwo"]
            dwo_b = dwo.count

            def P4_wld(q, b=b, dwo=dwo):
                t = dwo.inc(q.dma_start(
                    out=wcol[0][:].rearrange("p (kc m) -> p kc m", m=1024),
                    in_=W["wob"][0:256, :].rearrange(
                        "(kc p) m -> p kc m", p=128)), step=16)
                t = dwo.inc(q.dma_start(
                    out=wcol[1][:].rearrange("p (kc m) -> p kc m", m=1024),
                    in_=W["wob"][256:512, :].rearrange(
                        "(kc p) m -> p kc m", p=128)), step=16)
                K.sq.wait(t)
            K.Q(P4_wld)

            def P4_ld2(q, b=b, ld2=ld2, d2=d2):
                for t in range(NT_B):
                    if t >= 2:
                        K.sq.wait(p_app.at(app_b + (t - 2) + 1))
                    r0 = b * S + t * 128
                    ld2.append(d2[t % 2].inc(q.dma_start(
                        out=x2[t % 2][:], in_=x_in[r0:r0 + 128, :]),
                        step=16))
            K.Q(P4_ld2)

            def P4_mm(teng, b=b, dwo=dwo, dwo_b=dwo_b):
                K.te.wait(dwo.at(dwo_b + 32))
                for t in range(NT_B):
                    pair = (psS[0], psS[1]) if t % 2 == 0 else \
                           (psA[0], psA[1])
                    if t >= 2:
                        K.te.wait(p_epi.at(epi_b + 2 * (t - 2) + 2))
                    last = None
                    for half in range(2):
                        for kc in range(4):
                            last = teng.matmul(
                                pair[half][:],
                                ofm[:, kc * SB + t * 128:
                                    kc * SB + (t + 1) * 128],
                                wcol[kc // 2][:, (kc % 2) * 1024
                                              + half * 512:
                                              (kc % 2) * 1024
                                              + (half + 1) * 512],
                                start=(kc == 0), stop=False)
                        last = teng.matmul(
                            pair[half][:], K.ones_bf[:, 0:128],
                            borow[:, half * 512:(half + 1) * 512],
                            start=False, stop=True)
                        K.te.inc(last, psem=p_mm4)
            K.T(P4_mm)

            def P4_act(s, b=b):
                for t in range(NT_B):
                    sl = t % 2
                    st8 = stt[:, sl * 8:sl * 8 + 8]
                    pair = (psS[0], psS[1]) if t % 2 == 0 else \
                           (psA[0], psA[1])
                    K.se.wait(p_mm4.at(mm4_b + 2 * t + 2))
                    if t >= 2:
                        K.se.wait(p_app.at(app_b + (t - 2) + 1))
                    inst = s.activation(yt[:, sl * D:sl * D + 512],
                                        pair[0][:], AF.Copy, scale=1.0,
                                        accum_out=st8[:, 0:1])
                    K.se.inc(inst, psem=p_epi)
                    inst = s.activation(yt[:, sl * D + 512:sl * D + 1024],
                                        pair[1][:], AF.Copy, scale=1.0,
                                        accum_out=st8[:, 6:7])
                    K.se.inc(inst, psem=p_epi)
                    K.se.wait(p_var.at(var_b + t + 1))
                    t6 = K.se.inc(s.activation(st8[:, 4:5], st8[:, 3:4],
                                               AF.Ln, bias=K.eps_ap,
                                               scale=1.0))
                    K.se.wait(t6)
                    inst = s.activation(st8[:, 4:5], st8[:, 4:5], AF.Exp,
                                        bias=K.zero_ap, scale=-0.5)
                    K.se.inc(inst, psem=p_rstd)
                    K.se.wait(p_var2.at(var2_b + t + 1))
                    t6 = K.se.inc(s.activation(st8[:, 5:6], st8[:, 7:8],
                                               AF.Ln, bias=K.eps_ap,
                                               scale=1.0))
                    K.se.wait(t6)
                    inst = s.activation(st8[:, 5:6], st8[:, 5:6], AF.Exp,
                                        bias=K.zero_ap, scale=-0.5)
                    K.se.inc(inst, psem=p_rstd2)
            K.Sx(P4_act)

            def P4_dve(v, b=b, ld2=ld2):
                for t in range(NT_B):
                    sl = t % 2
                    st8 = stt[:, sl * 8:sl * 8 + 8]
                    ytile = yt[:, sl * D:sl * D + D]
                    ztile = zt[:, sl * D:sl * D + D]
                    K.ve.wait(p_epi.at(epi_b + 2 * t + 2))
                    K.ve.wait(K.ve.tok())
                    t1 = K.ve.inc(v.tensor_scalar(st8[:, 1:2], st8[:, 0:1],
                                                  1.0, st8[:, 6:7],
                                                  op0=ALU.mult,
                                                  op1=ALU.add))
                    K.ve.wait(t1)
                    t1 = K.ve.inc(v.tensor_scalar(st8[:, 1:2], st8[:, 1:2],
                                                  1.0 / D, None,
                                                  op0=ALU.mult))
                    t3 = K.ve.inc(v.scalar_tensor_tensor(
                        K.sqscr[:, 0:D], ytile, 1.0, ytile, op0=ALU.mult,
                        op1=ALU.mult, accum_out=st8[:, 2:3]))
                    K.ve.wait(t1, t3)
                    t4 = K.ve.inc(v.tensor_scalar(st8[:, 3:4], st8[:, 1:2],
                                                  st8[:, 1:2], None,
                                                  op0=ALU.mult))
                    K.ve.wait(t4)
                    inst = v.tensor_scalar(st8[:, 3:4], st8[:, 2:3], 1.0 / D,
                                           st8[:, 3:4], op0=ALU.mult,
                                           op1=ALU.subtract)
                    K.ve.inc(inst, psem=p_var)
                    K.ve.wait(p_rstd.at(rstd_b + t + 1))
                    t5 = K.ve.inc(v.tensor_scalar(ztile, ytile, st8[:, 1:2],
                                                  st8[:, 4:5],
                                                  op0=ALU.subtract,
                                                  op1=ALU.mult))
                    K.ve.wait(t5)
                    t5 = K.ve.inc(v.tensor_mul(ztile, ztile, gb[:]))
                    K.ve.wait(t5)
                    K.ve.inc(v.engine_nop(), psem=p_z2)
                    K.ve.wait(p_pool.at(pool_b + t + 1))
                    K.ve.wait(K.ve.tok())
                    t1 = K.ve.inc(v.reduce_sum(st8[:, 7:8], ztile,
                                               axis=AX.X))
                    t3 = K.ve.inc(v.scalar_tensor_tensor(
                        K.sqscr[:, 0:D], ztile, 1.0, ztile, op0=ALU.mult,
                        op1=ALU.mult, accum_out=st8[:, 2:3]))
                    K.ve.wait(t1, t3)
                    t1 = K.ve.inc(v.tensor_scalar(st8[:, 7:8], st8[:, 7:8],
                                                  1.0 / D, None,
                                                  op0=ALU.mult))
                    K.ve.wait(t1)
                    t4 = K.ve.inc(v.tensor_scalar(st8[:, 6:7], st8[:, 7:8],
                                                  st8[:, 7:8], None,
                                                  op0=ALU.mult))
                    K.ve.wait(t4)
                    inst = v.tensor_scalar(st8[:, 6:7], st8[:, 2:3], 1.0 / D,
                                           st8[:, 6:7], op0=ALU.mult,
                                           op1=ALU.subtract)
                    K.ve.wait(K.ve.inc(inst))
                    inst = v.tensor_copy(st8[:, 2:3], st8[:, 7:8])
                    K.ve.wait(K.ve.inc(inst))
                    inst = v.tensor_copy(st8[:, 7:8], st8[:, 6:7])
                    K.ve.inc(inst, psem=p_var2)
                    K.ve.wait(p_rstd2.at(rstd2_b + t + 1))
                    t5 = K.ve.inc(v.tensor_scalar(ztile, ztile, st8[:, 2:3],
                                                  st8[:, 5:6],
                                                  op0=ALU.subtract,
                                                  op1=ALU.mult))
                    K.ve.wait(t5, ld2[t])
                    if t >= 2:
                        K.ve.wait(dot[t % 2].at(dot_b[t % 2]
                                                + 16 * (t // 2)))
                    inst = v.tensor_add(ot[t % 2][:], ztile, x2[t % 2][:])
                    K.ve.inc(inst, psem=p_app)
            K.V(P4_dve)

            def P4_pool(g, b=b, dot=dot):
                tk = None
                for t in range(NT_B):
                    sl = t % 2
                    ztile = zt[:, sl * D:sl * D + D]
                    K.po.wait(p_z2.at(z2_b + t + 1))
                    t1 = K.po.inc(g.tensor_add(ztile, ztile, bb[:]))
                    K.po.wait(t1)
                    t1 = K.po.inc(g.tensor_add(ztile, ztile,
                                               xl[:, t * D:(t + 1) * D]))
                    K.po.wait(t1)
                    inst = g.tensor_scalar(ztile, ztile, RS2, None,
                                           op0=ALU.mult)
                    K.po.inc(inst, psem=p_pool)
                    if t >= 1:
                        K.po.wait(p_app.at(app_b + t))
                        r0 = b * S + (t - 1) * 128
                        tk = dot[(t - 1) % 2].inc(
                            g.dma_start(out=x_out[r0:r0 + 128, :],
                                        in_=ot[(t - 1) % 2][:]), step=16)
                K.po.wait(p_app.at(app_b + NT_B))
                r0 = b * S + (NT_B - 1) * 128
                tk = dot[(NT_B - 1) % 2].inc(
                    g.dma_start(out=x_out[r0:r0 + 128, :],
                                in_=ot[(NT_B - 1) % 2][:]), step=16)
                K.po.wait(tk)
            K.G(P4_pool)
            K.barrier()



def emit_l2_attn(K, es, x_in, x_out, W):
    nc = K.nc
    SB = S
    H, DH = 16, 64
    with ExitStack() as es2:
        DS = K.named_psems(es2, "a2d", ["dx0", "dx1", "dw0", "dw1",
                                        "d20", "d21", "dot0", "dot1",
                                        "dqh", "bld", "dpol", "dwo",
                                        "dmsk"])
        P = K.named_psems(es2, "a2", [
            "cons", "evac", "qk", "vt", "pol", "sq", "sqc", "sqm", "exp",
            "pvj", "zr", "bc", "oevac", "oev", "mm4", "epi", "var", "rstd",
            "app", "gs", "msk"])
        xt = [es2.enter_context(nc.sbuf_tensor(f"a2_xt{i}", [128, D], F32))
              for i in range(2)]
        x2 = xt
        xl = [es2.enter_context(nc.sbuf_tensor(f"a2_xl{i}", [128, D], BF16))
              for i in range(2)]
        xfm = es2.enter_context(nc.sbuf_tensor("a2_xfm", [128, DC * SB],
                                               BF16))
        qfm = es2.enter_context(nc.sbuf_tensor("a2_qfm", [128, DC * SB],
                                               BF16))
        kfm = es2.enter_context(nc.sbuf_tensor("a2_kfm", [128, DC * SB],
                                               BF16))
        vtm = es2.enter_context(nc.sbuf_tensor("a2_vtm",
                                               [128, NT_B * (H * 65)],
                                               BF16))
        ofm = xfm
        wov2 = qfm
        qhat = es2.enter_context(nc.sbuf_tensor("a2_qhat", [65, SB], BF16))
        khat = es2.enter_context(nc.sbuf_tensor("a2_khat", [65, SB], BF16))
        wcol = [es2.enter_context(nc.sbuf_tensor(f"a2_w{i}", [128, 4096],
                                                 BF16)) for i in range(2)]
        polb = es2.enter_context(nc.sbuf_tensor("a2_polb", [128, 16], F32))
        prow = es2.enter_context(nc.sbuf_tensor("a2_prow", [1, 128], F32))
        xmf = es2.enter_context(nc.sbuf_tensor("a2_xmf", [128, DC], F32))
        phid = es2.enter_context(nc.sbuf_tensor("a2_phid", [128, 16], F32))
        gcol = es2.enter_context(nc.sbuf_tensor("a2_gcol", [64, 1], F32))

        mval = es2.enter_context(nc.sbuf_tensor("a2_mval", [1, 1], F32))
        brows = es2.enter_context(nc.sbuf_tensor("a2_brows", [1, D], BF16))
        biasf = es2.enter_context(nc.sbuf_tensor("a2_biasf", [128, 16],
                                                 F32))
        borow = es2.enter_context(nc.sbuf_tensor("a2_bo", [1, 1024], BF16))
        et = [es2.enter_context(nc.sbuf_tensor(f"a2_et{i}", [128, 512],
                                               BF16)) for i in range(2)]
        zr = es2.enter_context(nc.sbuf_tensor("a2_zr", [1, 512], F32))
        obuf = es2.enter_context(nc.sbuf_tensor("a2_obuf", [64, 512], BF16))
        stt = es2.enter_context(nc.sbuf_tensor("a2_st", [128, 16], F32))
        yt = es2.enter_context(nc.sbuf_tensor("a2_yt", [128, 2 * D], F32))
        sq2 = yt[:].bitcast(BF16)
        polw = yt[:, 0:1024]
        nt_ = es2.enter_context(nc.sbuf_tensor("a2_nt", [128, 2 * D], F32))
        masks = nt_[:].bitcast(BF16)[:, 0:2048]
        ot_t = es2.enter_context(nc.sbuf_tensor("a2_ot", [128, 2 * D],
                                                F32))
        ot = [ot_t[:, 0:D], ot_t[:, D:2 * D]]
        nrm3 = ot_t[0:65, :]
        psA = [es2.enter_context(nc.psum_tensor(f"a2_psA{i}", [128, 512],
                                                F32)) for i in range(2)]
        psS = [es2.enter_context(nc.psum_tensor(f"a2_psS{i}", [128, 512],
                                                F32)) for i in range(2)]
        psOA = es2.enter_context(nc.psum_tensor("a2_psOA", [128, 512], F32))
        pst = [es2.enter_context(nc.psum_tensor(f"a2_pst{i}", [128, 512],
                                                BF16)) for i in range(2)]

        def B_ld(q):
            ds = DS["bld"]
            ds.inc(q.dma_start(out=biasf[:, 0:8], in_=W["bq"]), step=16)
            ds.inc(q.dma_start(out=biasf[:, 8:16], in_=W["bk"]), step=16)
            ds.inc(q.dma_start(out=polb[:, 0:4], in_=W["sb1c"]), step=16)
            ds.inc(q.dma_start(out=polb[:, 4:16], in_=W["pb1c"]), step=16)
            ds.inc(q.dma_start(out=prow[0:1, 0:48], in_=W["pb2r"]), step=16)
            t = ds.inc(q.dma_start(out=prow[0:1, 48:51], in_=W["sb2r"]),
                       step=16)
            K.sq.wait(t)
        K.Q(B_ld)

        def B_ldg(g):
            ds = DS["dqh"]
            ds.inc(g.dma_start(out=brows[0:1, 0:D], in_=W["bv"].rearrange(
                "(o d) -> o d", o=1)), step=16)
            t = ds.inc(g.dma_start(out=borow[:], in_=W["bo"].rearrange(
                "(o d) -> o d", o=1)), step=16)
            K.po.wait(t)
        K.G(B_ldg)

        def B_misc(v):
            K.ve.inc(v.memset(khat[64:65, :], -1.0))
            inst = v.memset(vtm[:].rearrange(
                "p (t h w) -> p t h w", t=NT_B, h=H)[:, :, :, 64:65], 1.0)
            t = K.ve.inc(inst)
            K.ve.wait(t)
        K.V(B_misc)
        K.barrier()

        for b in range(B_PER_CORE):
            p_cons = P["cons"]
            cons_b = p_cons.count
            p_evac = P["evac"]
            evac_b = p_evac.count
            dx = [DS["dx0"], DS["dx1"]]
            ld = []

            def P1_ld(q, b=b, ld=ld, p_cons=p_cons, cons_b=cons_b, dx=dx):
                for t in range(NT_B):
                    if t >= 2:
                        K.sq.wait(p_cons.at(cons_b + (t - 2) + 1))
                    r0 = b * S + t * 128
                    ld.append(dx[t % 2].inc(
                        q.dma_start(out=xt[t % 2][:],
                                    in_=x_in[r0:r0 + 128, :]), step=16))
            K.Q(P1_ld)

            var_toks = []
            ap_toks = []
            te_b1 = K.te.count

            def P1_ve(v, b=b, ld=ld, p_evac=p_evac, evac_b=evac_b,
                      te_b1=te_b1):
                for t in range(NT_B + 1):
                    if t < NT_B:
                        K.ve.wait(ld[t])
                        st8 = stt[:, (t % 2) * 8:(t % 2) * 8 + 8]
                        var_toks.append(ln_stats_v(K, v, xt[t % 2][:],
                                                   st8))
                    if t >= 1:
                        tt = t - 1
                        for half in range(2):
                            r = tt * 2 + half
                            K.ve.wait((K.te.sem, te_b1 + r + 1))
                            pb = pst[r % 2]
                            for j in range(4):
                                c = half * 4 + j
                                inst = v.tensor_copy(
                                    xfm[:, c * SB + tt * 128:
                                        c * SB + (tt + 1) * 128],
                                    pb[:, j * 128:(j + 1) * 128])
                                if j < 3:
                                    K.ve.inc(inst)
                            K.ve.inc(inst, psem=p_evac)
            K.V(P1_ve)

            def P1_s(s, b=b, p_cons=p_cons, te_b1=te_b1):
                for t in range(NT_B):
                    K.se.wait(var_toks[t])
                    if t >= 2:
                        K.se.wait((K.te.sem, te_b1 + 2 * (t - 2) + 2))
                    st8 = stt[:, (t % 2) * 8:(t % 2) * 8 + 8]
                    tA, _ = ln_apply_s(K, s, xt[t % 2][:], xl[t % 2][:],
                                       st8, extra_psem=p_cons)
                    ap_toks.append(tA)
            K.Sx(P1_s)

            def P1_tr(teng, b=b, p_evac=p_evac, evac_b=evac_b):
                for t in range(NT_B):
                    K.te.wait(ap_toks[t])
                    for half in range(2):
                        r = t * 2 + half
                        if r >= 2:
                            K.te.wait(p_evac.at(evac_b + (r - 2) + 1))
                        pb = pst[r % 2]
                        for j in range(4):
                            c = half * 4 + j
                            last = teng.transpose(
                                pb[:, j * 128:(j + 1) * 128],
                                xl[t % 2][:, c * 128:(c + 1) * 128],
                                K.ident[:])
                        K.te.inc(last)
            K.T(P1_tr)
            K.barrier()

            def POL_xm(v, b=b):
                for c in range(DC):
                    t = K.ve.inc(v.reduce_sum(xmf[:, c:c + 1],
                                              xfm[:, c * SB:(c + 1) * SB],
                                              axis=AX.X))
                K.ve.wait(t)
                t = K.ve.inc(v.tensor_scalar(xmf[:], xmf[:], 1.0 / SB,
                                             None, op0=ALU.mult))
                K.ve.wait(t)
            K.V(POL_xm)
            K.barrier()

            pol = P["pol"]
            l1_jobs = [("sw1T", 0)] + [(f"pw1T{p_}", 1 + p_)
                                       for p_ in range(3)]
            for wname, net in l1_jobs:
                for cc in range(4):
                    slot = net * 4 + cc

                    def ldc(q, wname=wname, cc=cc):
                        ds = DS["dpol"]
                        K.sq.wait(pol.at(pol.count))
                        ds.inc(q.dma_start(
                            out=polw[:].rearrange("p (kc m) -> p kc m",
                                                  m=128),
                            in_=W[wname][cc * 1024:(cc + 1) * 1024, :]
                            .rearrange("(kc p) m -> p kc m", p=128)),
                            step=16)
                    K.Q(ldc)

                    def mmc(teng, slot=slot):
                        K.te.wait(DS["dpol"].at(DS["dpol"].count))
                        for kc in range(DC):
                            last = teng.matmul(
                                psA[0][:, 0:1],
                                polw[:, kc * 128:(kc + 1) * 128],
                                xmf[:, kc:kc + 1],
                                start=(kc == 0), stop=(kc == DC - 1))
                        K.te.inc(last)
                    K.T(mmc)

                    def evc(s, slot=slot):
                        K.se.wait(K.te.tok())
                        inst = s.activation(phid[:, slot:slot + 1],
                                            psA[0][:, 0:1], AF.Gelu,
                                            bias=polb[:, slot:slot + 1],
                                            scale=1.0)
                        K.se.inc(inst, psem=pol)
                    K.Sx(evc)

            def ld2w(q):
                ds = DS["dpol"]
                K.sq.wait(pol.at(pol.count))
                ds.inc(q.dma_start(
                    out=polw[:, 0:4 * 51].rearrange("p (kc m) -> p kc m",
                                                    m=51),
                    in_=W["l2w"].rearrange("(kc p) m -> p kc m", p=128)),
                    step=16)
            K.Q(ld2w)

            def mm2(teng):
                K.te.wait(DS["dpol"].at(DS["dpol"].count))
                K.te.wait(pol.at(pol.count))
                for which in range(4):
                    c0 = 0 if which == 0 else 3 + (which - 1) * 16
                    cw = 3 if which == 0 else 16
                    src0 = 0 if which == 0 else 4 * which
                    for kc in range(4):
                        last = teng.matmul(
                            psA[1][0:1, which * 16:which * 16 + cw],
                            phid[:, src0 + kc:src0 + kc + 1],
                            polw[:, kc * 51 + c0:kc * 51 + c0 + cw],
                            start=(kc == 0), stop=(kc == 3))
                    K.te.inc(last)
            K.T(mm2)

            def g_v1(v):
                K.ve.wait(K.te.tok())
                t = K.ve.inc(v.tensor_add(prow[0:1, 96:99],
                                          psA[1][0:1, 0:3],
                                          prow[0:1, 48:51]))
                K.ve.wait(t)
                t = K.ve.inc(v.reduce_max(prow[0:1, 51:52],
                                          prow[0:1, 96:99], axis=AX.X,
                                          negate=True))
                K.ve.wait(t)
            K.V(g_v1)

            def g_s1(s):
                K.se.wait(K.ve.tok())
                inst = s.activation(prow[0:1, 96:99], prow[0:1, 96:99],
                                    AF.Exp, bias=prow[0:1, 51:52],
                                    scale=1.0, accum_out=prow[0:1, 52:53])
                K.se.inc(inst)
            K.Sx(g_s1)

            def g_v2(v):
                K.ve.wait(K.se.tok())
                t = K.ve.inc(v.reciprocal(prow[0:1, 52:53],
                                          prow[0:1, 52:53]))
                K.ve.wait(t)
                t = K.ve.inc(v.tensor_scalar(prow[0:1, 96:99],
                                             prow[0:1, 96:99],
                                             prow[0:1, 52:53], None,
                                             op0=ALU.mult))
                K.ve.wait(t)
                t = K.ve.inc(v.memset(prow[0:1, 80:96], 0.0))
                K.ve.wait(t)
            K.V(g_v2)
            for p_ in range(3):
                def g_v3(v, p_=p_):
                    K.ve.wait(K.se.tok())
                    t = K.ve.inc(v.tensor_add(
                        prow[0:1, 64:80],
                        psA[1][0:1, 16 * (p_ + 1):16 * (p_ + 1) + 16],
                        prow[0:1, p_ * 16:p_ * 16 + 16]))
                    K.ve.wait(t)
                    t = K.ve.inc(v.reduce_max(prow[0:1, 51:52],
                                              prow[0:1, 64:80],
                                              axis=AX.X, negate=True))
                    K.ve.wait(t)
                K.V(g_v3)

                def g_s2(s, p_=p_):
                    K.se.wait(K.ve.tok())
                    inst = s.activation(prow[0:1, 64:80],
                                        prow[0:1, 64:80], AF.Exp,
                                        bias=prow[0:1, 51:52], scale=1.0,
                                        accum_out=prow[0:1, 52:53])
                    K.se.inc(inst)
                K.Sx(g_s2)

                def g_v4(v, p_=p_):
                    K.ve.wait(K.se.tok())
                    t = K.ve.inc(v.reciprocal(prow[0:1, 52:53],
                                              prow[0:1, 52:53]))
                    K.ve.wait(t)
                    t = K.ve.inc(v.tensor_scalar(prow[0:1, 64:80],
                                                 prow[0:1, 64:80],
                                                 prow[0:1, 52:53],
                                                 prow[0:1, 96 + p_:
                                                      97 + p_],
                                                 op0=ALU.mult,
                                                 op1=ALU.mult))
                    K.ve.wait(t)
                    t = K.ve.inc(v.tensor_add(prow[0:1, 80:96],
                                              prow[0:1, 80:96],
                                              prow[0:1, 64:80]))
                    K.ve.wait(t)
                K.V(g_v4)

            def g_v5(v):
                t = K.ve.inc(v.tensor_scalar(prow[0:1, 80:96],
                                             prow[0:1, 80:96], 0.125,
                                             None, op0=ALU.mult))
                K.ve.wait(t)
            K.V(g_v5)
            K.barrier()

            pol = P["pol"]
            l1_jobs = [("sw1T", 0)] + [(f"pw1T{p_}", 1 + p_)
                                       for p_ in range(3)]
            for wname, net in l1_jobs:
                for cc in range(4):
                    slot = net * 4 + cc

                    def ldc(q, wname=wname, cc=cc):
                        ds = DS["dpol"]
                        K.sq.wait(pol.at(pol.count))
                        ds.inc(q.dma_start(
                            out=polw[:].rearrange("p (kc m) -> p kc m",
                                                  m=128),
                            in_=W[wname][cc * 1024:(cc + 1) * 1024, :]
                            .rearrange("(kc p) m -> p kc m", p=128)),
                            step=16)
                    K.Q(ldc)

                    def mmc(teng, slot=slot):
                        K.te.wait(DS["dpol"].at(DS["dpol"].count))
                        for kc in range(DC):
                            last = teng.matmul(
                                psA[0][:, 0:1],
                                polw[:, kc * 128:(kc + 1) * 128],
                                xmf[:, kc:kc + 1],
                                start=(kc == 0), stop=(kc == DC - 1))
                        K.te.inc(last)
                    K.T(mmc)

                    def evc(s, slot=slot):
                        K.se.wait(K.te.tok())
                        inst = s.activation(phid[:, slot:slot + 1],
                                            psA[0][:, 0:1], AF.Gelu,
                                            bias=polb[:, slot:slot + 1],
                                            scale=1.0)
                        K.se.inc(inst, psem=pol)
                    K.Sx(evc)

            def ld2(q):
                ds = DS["dpol"]
                K.sq.wait(pol.at(pol.count))
                ds.inc(q.dma_start(
                    out=polw[:, 0:4 * 51].rearrange("p (kc m) -> p kc m",
                                                    m=51),
                    in_=W["l2w"].rearrange("(kc p) m -> p kc m", p=128)),
                    step=16)
            K.Q(ld2)

            def mm2(teng):
                K.te.wait(DS["dpol"].at(DS["dpol"].count))
                K.te.wait(pol.at(pol.count))
                for which in range(4):
                    c0 = 0 if which == 0 else 3 + (which - 1) * 16
                    cw = 3 if which == 0 else 16
                    src0 = 0 if which == 0 else 4 * which
                    for kc in range(4):
                        last = teng.matmul(
                            psA[1][0:1, which * 16:which * 16 + cw],
                            phid[:, src0 + kc:src0 + kc + 1],
                            polw[:, kc * 51 + c0:kc * 51 + c0 + cw],
                            start=(kc == 0), stop=(kc == 3))
                    K.te.inc(last)
            K.T(mm2)

            def POL_fin_v(v, b=b):
                K.ve.wait(K.te.tok())
                t = K.ve.inc(v.tensor_add(prow[0:1, 96:99],
                                          psA[1][0:1, 0:3],
                                          prow[0:1, 48:51]))
                for p_ in range(3):
                    t = K.ve.inc(v.tensor_add(
                        prow[0:1, 100 + 0:100 + 16] if False else
                        prow[0:1, 64:80],
                        psA[1][0:1, 16 * (p_ + 1):16 * (p_ + 1) + 16],
                        prow[0:1, p_ * 16:p_ * 16 + 16]))
                    K.ve.wait(t)
                    t = K.ve.inc(v.tensor_copy(
                        prow[0:1, 100 + 0:100 + 16] if False else
                        prow[0:1, 80 + 0:80 + 16] if p_ == 99 else
                        prow[0:1, 100:116] if p_ == 98 else
                        prow[0:1, 51 + p_ * 16:51 + p_ * 16 + 16]
                        if False else prow[0:1, 64:80], prow[0:1, 64:80]))
                K.ve.wait(t)
            def POL_gates(b=b):
                def v1(v):
                    K.ve.wait(K.te.tok())
                    t = K.ve.inc(v.tensor_add(prow[0:1, 96:99],
                                              psA[1][0:1, 0:3],
                                              prow[0:1, 48:51]))
                    K.ve.wait(t)
                    t = K.ve.inc(v.reduce_max(prow[0:1, 51:52],
                                              prow[0:1, 96:99], axis=AX.X,
                                              negate=True))
                    K.ve.wait(t)
                K.V(v1)

                def s1(s):
                    K.se.wait(K.ve.tok())
                    inst = s.activation(prow[0:1, 96:99], prow[0:1, 96:99],
                                        AF.Exp, bias=prow[0:1, 51:52],
                                        scale=1.0,
                                        accum_out=prow[0:1, 52:53])
                    K.se.inc(inst)
                K.Sx(s1)

                def v2(v):
                    K.ve.wait(K.se.tok())
                    t = K.ve.inc(v.reciprocal(prow[0:1, 52:53],
                                              prow[0:1, 52:53]))
                    K.ve.wait(t)
                    t = K.ve.inc(v.tensor_scalar(prow[0:1, 96:99],
                                                 prow[0:1, 96:99],
                                                 prow[0:1, 52:53], None,
                                                 op0=ALU.mult))
                    K.ve.wait(t)
                    t = K.ve.inc(v.memset(prow[0:1, 80:96], 0.0))
                    K.ve.wait(t)
                K.V(v2)
                for p_ in range(3):
                    def v3(v, p_=p_):
                        K.ve.wait(K.se.tok())
                        t = K.ve.inc(v.tensor_add(
                            prow[0:1, 64:80],
                            psA[1][0:1, 16 * (p_ + 1):16 * (p_ + 1) + 16],
                            prow[0:1, p_ * 16:p_ * 16 + 16]))
                        K.ve.wait(t)
                        t = K.ve.inc(v.reduce_max(prow[0:1, 51:52],
                                                  prow[0:1, 64:80],
                                                  axis=AX.X, negate=True))
                        K.ve.wait(t)
                    K.V(v3)

                    def s2(s, p_=p_):
                        K.se.wait(K.ve.tok())
                        inst = s.activation(prow[0:1, 64:80],
                                            prow[0:1, 64:80], AF.Exp,
                                            bias=prow[0:1, 51:52],
                                            scale=1.0,
                                            accum_out=prow[0:1, 52:53])
                        K.se.inc(inst)
                    K.Sx(s2)

                    def v4(v, p_=p_):
                        K.ve.wait(K.se.tok())
                        t = K.ve.inc(v.reciprocal(prow[0:1, 52:53],
                                                  prow[0:1, 52:53]))
                        K.ve.wait(t)
                        t = K.ve.inc(v.tensor_scalar(prow[0:1, 64:80],
                                                     prow[0:1, 64:80],
                                                     prow[0:1, 52:53],
                                                     prow[0:1,
                                                          96 + p_:97 + p_],
                                                     op0=ALU.mult,
                                                     op1=ALU.mult))
                        K.ve.wait(t)
                        t = K.ve.inc(v.tensor_add(prow[0:1, 80:96],
                                                  prow[0:1, 80:96],
                                                  prow[0:1, 64:80]))
                        K.ve.wait(t)
                    K.V(v4)

                def v5(v):
                    t = K.ve.inc(v.tensor_scalar(prow[0:1, 80:96],
                                                 prow[0:1, 80:96], 0.125,
                                                 None, op0=ALU.mult))
                    K.ve.wait(t)
                K.V(v5)
            POL_gates()
            K.barrier()

            dw = [DS["dw0"], DS["dw1"]]
            p_qk = P["qk"]
            qk_b = p_qk.count

            projs = [("wqb", qfm, 0), ("wkb", kfm, 8)]
            for wnm, dstfm, boff in projs:
                dw_b = [d.count for d in dw]
                base_r = p_qk.count

                def PR_ld(q, wnm=wnm, dw_b=dw_b, base_r=base_r):
                    for mc in range(8):
                        if mc < 2:
                            K.sq.wait(p_qk.at(base_r))
                        else:
                            K.sq.wait(p_qk.at(base_r + 4 * (mc - 1)))
                        r0 = mc * DC * 128
                        dw[mc % 2].inc(q.dma_start(
                            out=wcol[mc % 2][:, 0:DC * 128].rearrange(
                                "p (kc m) -> p kc m", m=128),
                            in_=W[wnm][r0:r0 + DC * 128, :].rearrange(
                                "(kc p) m -> p kc m", p=128)), step=16)
                K.Q(PR_ld)

                mm_t = {}

                def PR_mm(teng, wnm=wnm, dw_b=dw_b, base_r=base_r,
                          mm_t=mm_t):
                    for mc in range(8):
                        K.te.wait(dw[mc % 2].at(dw_b[mc % 2]
                                                + 16 * (mc // 2 + 1)))
                        for sp4 in range(4):
                            r = mc * 4 + sp4
                            if r >= 2:
                                K.te.wait(p_qk.at(base_r + r - 1))
                            pa = psA[r % 2]
                            for kc in range(DC):
                                last = teng.matmul(
                                    pa[:],
                                    wcol[mc % 2][:, kc * 128:(kc + 1) * 128],
                                    xfm[:, kc * SB + sp4 * 512:
                                        kc * SB + (sp4 + 1) * 512],
                                    start=(kc == 0), stop=(kc == DC - 1))
                            mm_t[(mc, sp4)] = K.te.inc(last)
                K.T(PR_mm)

                def PR_ev(s, dstfm=dstfm, boff=boff, mm_t=mm_t):
                    for mc in range(8):
                        for sp4 in range(4):
                            r = mc * 4 + sp4
                            K.se.wait(mm_t[(mc, sp4)])
                            inst = s.activation(
                                dstfm[:, mc * SB + sp4 * 512:
                                      mc * SB + (sp4 + 1) * 512],
                                psA[r % 2][:], AF.Identity,
                                bias=biasf[:, boff + mc:boff + mc + 1],
                                scale=1.0)
                            K.se.inc(inst, psem=p_qk)
                K.Sx(PR_ev)

            p_vt = P["vt"]
            vt_b = p_vt.count
            dv_b = [d.count for d in dw]

            def PV_ld(q, b=b, dv_b=dv_b):
                K.sq.wait(p_qk.at(p_qk.count))
                for hh in range(2):
                    dw[hh].inc(q.dma_start(
                        out=wcol[hh][:].rearrange("p (kc m) -> p kc m",
                                                  m=512),
                        in_=W[f"wvT{hh}"].rearrange(
                            "(kc p) m -> p kc m", p=128)), step=16)
            K.Q(PV_ld)
            vmm = {}

            def PV_mm(teng, b=b, dv_b=dv_b, vmm=vmm):
                K.te.wait(dw[0].at(dv_b[0] + 16))
                K.te.wait(dw[1].at(dv_b[1] + 16))
                for t in range(NT_B):
                    for hh in range(2):
                        r = t * 2 + hh
                        if r >= 2:
                            K.te.wait(p_vt.at(vt_b + r - 1))
                        pa = psS[r % 2]
                        for kc in range(DC):
                            wsrc = wcol[hh][:, kc * 512:(kc + 1) * 512]
                            teng.matmul(
                                pa[:],
                                xfm[:, kc * SB + t * 128:
                                    kc * SB + (t + 1) * 128],
                                wsrc, start=(kc == 0), stop=False)
                        last = teng.matmul(
                            pa[:], K.ones_bf[:, 0:128],
                            brows[0:1, hh * 512:(hh + 1) * 512],
                            start=False, stop=True)
                        vmm[(t, hh)] = K.te.inc(last)
            K.T(PV_mm)

            def PV_ev(v, b=b, vmm=vmm):
                for t in range(NT_B):
                    for hh in range(2):
                        K.ve.wait(vmm[(t, hh)])
                        for hp in range(8):
                            hg = hh * 8 + hp
                            inst = v.tensor_copy(
                                vtm[:, t * (H * 65) + hg * 65:
                                    t * (H * 65) + hg * 65 + 64],
                                psS[hh][:, hp * 64:(hp + 1) * 64])
                            if hp < 7:
                                K.ve.inc(inst)
                        K.ve.inc(inst, psem=p_vt)
            K.V(PV_ev)
            K.barrier()

            def P3_mld(g, b=b):
                ds = DS["dmsk"]
                t = ds.inc(g.dma_start(out=masks[:], in_=W["mask4"]),
                           step=16)
                K.po.wait(t)
            K.G(P3_mld)
            K.barrier()
            prev_head_done = None
            for h in range(H):
                dqh = DS["dqh"]
                dqh_b = dqh.count

                def P3_build(g, b=b, h=h, dqh=dqh,
                             prev_head_done=prev_head_done):
                    if prev_head_done is not None:
                        K.po.wait(prev_head_done)
                    K.po.wait(P["gs"].at(P["gs"].count))
                    qsrc = qfm[(h % 2) * 64:(h % 2) * 64 + 64,
                               (h // 2) * SB:(h // 2 + 1) * SB]
                    ksrc = kfm[(h % 2) * 64:(h % 2) * 64 + 64,
                               (h // 2) * SB:(h // 2 + 1) * SB]
                    dqh.inc(g.dma_start(out=qhat[0:64, :], in_=qsrc),
                            step=16)
                    t = dqh.inc(g.dma_start(out=khat[0:64, :], in_=ksrc),
                                step=16)
                    K.po.wait(t)
                K.G(P3_build)
                bld_tok = (dqh.h, dqh_b + 32)

                def P3_gs_mm(teng, b=b, h=h, bld_tok=bld_tok):
                    K.te.wait(bld_tok)
                    K.te.wait(K.ve.tok())
                    inst = teng.matmul(psOA[0:64, 0:1],
                                       K.ones_f32[:, 0:64],
                                       prow[0:1, 80 + h:81 + h],
                                       start=True, stop=True)
                    K.te.inc(inst)
                K.T(P3_gs_mm)

                def P3_gs_v(v, b=b, h=h):
                    K.ve.wait(K.te.tok())
                    t = K.ve.inc(v.tensor_copy(gcol[:], psOA[0:64, 0:1]))
                    K.ve.wait(t)
                    inst = v.tensor_scalar(qhat[0:64, :], qhat[0:64, :],
                                           gcol[0:64, 0:1], None,
                                           op0=ALU.mult)
                    K.ve.inc(inst, psem=P["gs"])
                K.V(P3_gs_v)
                gs_tok = (P["gs"].h, P["gs"].count)

                p_sq = P["sq"]
                sq_b = p_sq.count
                p_sqm = P["sqm"]
                sqm_b = p_sqm.count
                p_sqc = P["sqc"]
                sqc_b = p_sqc.count

                def P3_sq_s(s, b=b, h=h, gs_tok=gs_tok):
                    K.se.wait(gs_tok)
                    inst = s.square(sq2[0:64, 0:SB], qhat[0:64, :])
                    K.se.inc(inst, psem=p_sq)
                    K.se.wait(p_sqm.at(sqm_b + 4))
                    inst = s.square(sq2[0:64, 0:SB], khat[0:64, :])
                    K.se.inc(inst, psem=p_sq)
                K.Sx(P3_sq_s)

                def P3_sq_mm(teng, b=b, h=h):
                    for which in range(2):
                        K.te.wait(p_sq.at(sq_b + which + 1))
                        for sp4 in range(4):
                            r = which * 4 + sp4
                            if r >= 1:
                                K.te.wait(p_sqc.at(sqc_b + r))
                            inst = teng.matmul(
                                psOA[0:1, 0:512], K.ones_col[0:64, 0:1],
                                sq2[0:64, sp4 * 512:(sp4 + 1) * 512],
                                start=True, stop=True)
                            K.te.inc(inst, psem=p_sqm)
                K.T(P3_sq_mm)

                def P3_sq_ev(v, b=b, h=h):
                    for which in range(2):
                        for sp4 in range(4):
                            r = which * 4 + sp4
                            K.ve.wait(p_sqm.at(sqm_b + r + 1))
                            row = 0 if which == 0 else 32
                            inst = v.tensor_copy(
                                nrm3[row:row + 1,
                                     sp4 * 512:(sp4 + 1) * 512],
                                psOA[0:1, 0:512])
                            K.ve.inc(inst, psem=p_sqc)
                K.V(P3_sq_ev)

                def P3_norm_s(s, b=b, h=h):
                    K.se.wait(p_sqc.at(sqc_b + 8))
                    t = K.se.inc(s.activation(nrm3[0:1, :], nrm3[0:1, :],
                                              AF.Sqrt,
                                              bias=K.zero_ap[0:1, 0:1],
                                              scale=1.0))
                    K.se.wait(t)
                    t = K.se.inc(s.activation(nrm3[32:33, :],
                                              nrm3[32:33, :], AF.Sqrt,
                                              bias=K.zero_ap[0:1, 0:1],
                                              scale=1.0))
                    K.se.wait(t)
                K.Sx(P3_norm_s)

                def P3_m_v(v, b=b, h=h):
                    K.ve.wait(K.se.tok())
                    t = K.ve.inc(v.reduce_max(mval[:], nrm3[32:33, :],
                                              axis=AX.X))
                    K.ve.wait(t)
                    t = K.ve.inc(v.tensor_scalar(nrm3[64:65, :],
                                                 nrm3[0:1, :],
                                                 mval[0:1, 0:1], None,
                                                 op0=ALU.mult))
                    K.ve.wait(t)
                    t = K.ve.inc(v.tensor_copy(qhat[64:65, :],
                                               nrm3[64:65, :]))
                    K.ve.wait(t)
                K.V(P3_m_v)
                hdr_tok = K.ve.tok()

                p_exp = P["exp"]
                exp_b = p_exp.count
                p_pvj = P["pvj"]
                pvj_b = p_pvj.count
                p_zr = P["zr"]
                zr_b = p_zr.count
                p_bc = P["bc"]
                bc_b = p_bc.count
                p_oevac = P["oevac"]
                oevac_b = p_oevac.count
                p_oev = P["oev"]
                oev_b = p_oev.count
                p_msk = P["msk"]
                msk_b = p_msk.count
                smm = {}
                NJ = [4 * (sp4 + 1) for sp4 in range(4)]
                RBASE = [sum(NJ[:i]) for i in range(5)]

                def pv_gate(sp4, j):
                    nj = NJ[sp4]
                    if j >= nj - 4:
                        return p_msk.at(msk_b + sp4 * 4 + (j - (nj - 4))
                                        + 1)
                    return p_exp.at(exp_b + RBASE[sp4] + j + 1)

                def P3_te(teng, b=b, h=h, hdr_tok=hdr_tok, smm=smm):
                    K.te.wait(hdr_tok)
                    for sp4 in range(4):
                        nj = NJ[sp4]
                        for j in range(nj):
                            r = RBASE[sp4] + j
                            if r >= 2:
                                K.te.wait(p_exp.at(exp_b + r - 1))
                            inst = teng.matmul(
                                psS[r % 2][:],
                                khat[:, j * 128:(j + 1) * 128],
                                qhat[:, sp4 * 512:(sp4 + 1) * 512],
                                start=True, stop=True)
                            smm[(sp4, j)] = K.te.inc(inst)
                            if j >= 1:
                                jj = j - 1
                                K.te.wait(pv_gate(sp4, jj))
                                if jj == 0 and sp4 >= 1:
                                    K.te.wait(p_oev.at(oev_b + sp4))
                                inst = teng.matmul(
                                    psOA[0:65, :],
                                    vtm[:, jj * (H * 65) + h * 65:
                                        jj * (H * 65) + h * 65 + 65],
                                    et[jj % 2][:],
                                    start=(jj == 0), stop=False)
                                K.te.inc(inst, psem=p_pvj)
                        jj = nj - 1
                        K.te.wait(pv_gate(sp4, jj))
                        if jj == 0 and sp4 >= 1:
                            K.te.wait(p_oev.at(oev_b + sp4))
                        inst = teng.matmul(
                            psOA[0:65, :],
                            vtm[:, jj * (H * 65) + h * 65:
                                jj * (H * 65) + h * 65 + 65],
                            et[jj % 2][:], start=(jj == 0), stop=True)
                        K.te.inc(inst, psem=p_pvj)
                        K.te.wait(p_zr.at(zr_b + sp4 + 1))
                        if sp4 >= 1:
                            K.te.wait(p_oev.at(oev_b + sp4))
                        inst = teng.matmul(psA[0][0:64, 0:512],
                                           K.ones_f32[:, 0:64],
                                           zr[:], start=True, stop=True)
                        K.te.inc(inst, psem=p_bc)
                K.T(P3_te)

                def P3_exp(s, b=b, h=h, smm=smm):
                    for sp4 in range(4):
                        nj = NJ[sp4]
                        for j in range(nj):
                            r = RBASE[sp4] + j
                            K.se.wait(smm[(sp4, j)])
                            if r >= 2:
                                K.se.wait(p_pvj.at(pvj_b + r - 1))
                            inst = s.activation(et[j % 2][:],
                                                psS[r % 2][:], AF.Exp,
                                                bias=K.zero_ap, scale=1.0)
                            K.se.inc(inst, psem=p_exp)
                        K.se.wait(p_pvj.at(pvj_b + RBASE[sp4 + 1]))
                        if sp4 >= 1:
                            K.se.wait(p_oev.at(oev_b + sp4))
                        inst = s.activation(obuf[:], psOA[0:64, :],
                                            AF.Copy, scale=1.0)
                        K.se.inc(inst, psem=p_oevac)
                K.Sx(P3_exp)

                def P3_mask(g, b=b, h=h):
                    for sp4 in range(4):
                        nj = NJ[sp4]
                        for j in range(nj - 4, nj):
                            r = RBASE[sp4] + j
                            dlt = j - (nj - 4)
                            K.po.wait(p_exp.at(exp_b + r + 1))
                            inst = g.tensor_mul(
                                et[j % 2][:], et[j % 2][:],
                                masks[:, dlt * 512:(dlt + 1) * 512])
                            K.po.inc(inst, psem=p_msk)
                K.G(P3_mask)

                def P3_odve(v, b=b, h=h):
                    for sp4 in range(4):
                        K.ve.wait(p_pvj.at(pvj_b + RBASE[sp4 + 1]))
                        inst = v.reciprocal(zr[:], psOA[64:65, 0:512])
                        K.ve.inc(inst, psem=p_zr)
                        K.ve.wait(p_bc.at(bc_b + sp4 + 1))
                        K.ve.wait(p_oevac.at(oevac_b + sp4 + 1))
                        dst = ofm[(h % 2) * 64:(h % 2) * 64 + 64,
                                  (h // 2) * SB + sp4 * 512:
                                  (h // 2) * SB + (sp4 + 1) * 512]
                        inst = v.tensor_tensor(dst, obuf[:],
                                               psA[0][0:64, 0:512],
                                               op=ALU.mult)
                        K.ve.inc(inst, psem=p_oev)
                K.V(P3_odve)
                prev_head_done = (p_oev.h, oev_b + 4)
            K.barrier()

            p_mm4 = P["mm4"]
            mm4_b = p_mm4.count
            p_epi = P["epi"]
            epi_b = p_epi.count
            p_var = P["var"]
            var_b = p_var.count
            p_rstd = P["rstd"]
            rstd_b = p_rstd.count
            p_app = P["app"]
            app_b = p_app.count
            d2 = [DS["d20"], DS["d21"]]
            dot = [DS["dot0"], DS["dot1"]]
            dot_b = [d.count for d in dot]
            dwo = DS["dwo"]
            dwo_b = dwo.count
            ld2 = []

            def P4_wld(q, b=b, dwo=dwo):
                t = dwo.inc(q.dma_start(
                    out=wov2[:, 0:8192].rearrange("p (kc m) -> p kc m",
                                                  m=1024),
                    in_=W["woT"].rearrange("(kc p) m -> p kc m", p=128)),
                    step=16)
                K.sq.wait(t)
            K.Q(P4_wld)

            def P4_ld2(q, b=b, ld2=ld2, d2=d2):
                for t in range(NT_B):
                    if t >= 2:
                        K.sq.wait(p_app.at(app_b + (t - 2) + 1))
                    r0 = b * S + t * 128
                    ld2.append(d2[t % 2].inc(q.dma_start(
                        out=x2[t % 2][:], in_=x_in[r0:r0 + 128, :]),
                        step=16))
            K.Q(P4_ld2)

            def P4_mm(teng, b=b, dwo=dwo, dwo_b=dwo_b):
                K.te.wait(dwo.at(dwo_b + 16))
                for t in range(NT_B):
                    pair = (psS[0], psS[1]) if t % 2 == 0 else \
                           (psA[0], psA[1])
                    if t >= 2:
                        K.te.wait(p_epi.at(epi_b + 2 * (t - 2) + 2))
                    for half in range(2):
                        for kc in range(8):
                            last = teng.matmul(
                                pair[half][:],
                                ofm[:, kc * SB + t * 128:
                                    kc * SB + (t + 1) * 128],
                                wov2[:, kc * 1024 + half * 512:
                                     kc * 1024 + (half + 1) * 512],
                                start=(kc == 0), stop=False)
                        last = teng.matmul(
                            pair[half][:], K.ones_bf[:, 0:128],
                            borow[:, half * 512:(half + 1) * 512],
                            start=False, stop=True)
                        K.te.inc(last, psem=p_mm4)
            K.T(P4_mm)

            def P4_act(s, b=b):
                for t in range(NT_B):
                    sl = t % 2
                    st8 = stt[:, sl * 8:sl * 8 + 8]
                    pair = (psS[0], psS[1]) if t % 2 == 0 else \
                           (psA[0], psA[1])
                    K.se.wait(p_mm4.at(mm4_b + 2 * t + 2))
                    if t >= 2:
                        K.se.wait(p_app.at(app_b + (t - 2) + 1))
                    inst = s.activation(yt[:, sl * D:sl * D + 512],
                                        pair[0][:], AF.Copy, scale=1.0,
                                        accum_out=st8[:, 0:1])
                    K.se.inc(inst, psem=p_epi)
                    inst = s.activation(yt[:, sl * D + 512:sl * D + 1024],
                                        pair[1][:], AF.Copy, scale=1.0,
                                        accum_out=st8[:, 6:7])
                    K.se.inc(inst, psem=p_epi)
                    K.se.wait(p_var.at(var_b + t + 1))
                    t6 = K.se.inc(s.activation(st8[:, 4:5], st8[:, 3:4],
                                               AF.Ln, bias=K.eps_ap,
                                               scale=1.0))
                    K.se.wait(t6)
                    inst = s.activation(st8[:, 4:5], st8[:, 4:5], AF.Exp,
                                        bias=K.zero_ap, scale=-0.5)
                    K.se.inc(inst, psem=p_rstd)
            K.Sx(P4_act)

            def P4_dve(v, b=b, ld2=ld2):
                for t in range(NT_B):
                    sl = t % 2
                    st8 = stt[:, sl * 8:sl * 8 + 8]
                    ytile = yt[:, sl * D:sl * D + D]
                    K.ve.wait(p_epi.at(epi_b + 2 * t + 2))
                    K.ve.wait(K.ve.tok())
                    t1 = K.ve.inc(v.tensor_scalar(st8[:, 1:2], st8[:, 0:1],
                                                  1.0, st8[:, 6:7],
                                                  op0=ALU.mult,
                                                  op1=ALU.add))
                    K.ve.wait(t1)
                    t1 = K.ve.inc(v.tensor_scalar(st8[:, 1:2], st8[:, 1:2],
                                                  1.0 / D, None,
                                                  op0=ALU.mult))
                    t3 = K.ve.inc(v.scalar_tensor_tensor(
                        K.sqscr[:, 0:D], ytile, 1.0, ytile, op0=ALU.mult,
                        op1=ALU.mult, accum_out=st8[:, 2:3]))
                    K.ve.wait(t1, t3)
                    t4 = K.ve.inc(v.tensor_scalar(st8[:, 3:4], st8[:, 1:2],
                                                  st8[:, 1:2], None,
                                                  op0=ALU.mult))
                    K.ve.wait(t4)
                    inst = v.tensor_scalar(st8[:, 3:4], st8[:, 2:3],
                                           1.0 / D, st8[:, 3:4],
                                           op0=ALU.mult, op1=ALU.subtract)
                    K.ve.inc(inst, psem=p_var)
                    K.ve.wait(p_rstd.at(rstd_b + t + 1))
                    ntile = nt_[:, sl * D:sl * D + D]
                    t5 = K.ve.inc(v.tensor_scalar(ntile, ytile, st8[:, 1:2],
                                                  st8[:, 4:5],
                                                  op0=ALU.subtract,
                                                  op1=ALU.mult))
                    K.ve.wait(t5, ld2[t])
                    if t >= 2:
                        K.ve.wait(dot[t % 2].at(dot_b[t % 2]
                                                + 16 * (t // 2)))
                    inst = v.tensor_add(ot[t % 2], ntile, x2[t % 2][:])
                    K.ve.inc(inst, psem=p_app)
            K.V(P4_dve)

            def P4_st(g, b=b, dot=dot):
                for t in range(NT_B):
                    K.po.wait(p_app.at(app_b + t + 1))
                    r0 = b * S + t * 128
                    tk = dot[t % 2].inc(
                        g.dma_start(out=x_out[r0:r0 + 128, :],
                                    in_=ot[t % 2]), step=16)
                K.po.wait(tk)
            K.G(P4_st)
            K.barrier()



def emit_moe_ffn(K, es, x_in, x_out, W, kind):
    nc = K.nc
    ST = 384
    NST = (TOK + ST - 1) // ST
    EC = 4
    E = 16
    NSH = 1 if kind == "l2f" else 2
    NE = E + NSH
    with ExitStack() as es2:
        DS = K.named_psems(es2, kind + "d", [
            "dx0", "dx1", "dg", "dup0", "dup1", "dwd0", "dwd1",
            "dcw", "dcr", "d20", "d21", "dot0", "dot1"])
        P = K.named_psems(es2, kind + "p", [
            "cons", "evac", "rt", "cmb", "cw", "up", "ev", "mul", "bc",
            "dn", "epi", "var", "rstd", "app", "psa"])
        xt = [es2.enter_context(nc.sbuf_tensor(f"{kind}_xt{i}", [128, D],
                                               F32)) for i in range(2)]
        x2 = xt
        xl3 = es2.enter_context(nc.sbuf_tensor(f"{kind}_xl", [128, 3 * D],
                                               BF16))
        xfm = es2.enter_context(nc.sbuf_tensor(f"{kind}_xfm",
                                               [128, DC * ST], BF16))
        gwt = es2.enter_context(nc.sbuf_tensor(f"{kind}_gw", [128, 8 * 16],
                                               BF16))
        gbb = es2.enter_context(nc.sbuf_tensor(f"{kind}_gbb", [128, 16],
                                               F32))
        gbr = es2.enter_context(nc.sbuf_tensor(f"{kind}_gbr", [1, 16],
                                               F32))
        gbrb = es2.enter_context(nc.sbuf_tensor(f"{kind}_gbrb", [1, 16],
                                                BF16))
        rt3 = es2.enter_context(nc.sbuf_tensor(f"{kind}_rt", [128, 3 * 16],
                                               F32))
        cmb = es2.enter_context(nc.sbuf_tensor(f"{kind}_cmb",
                                               [128, 3 * 16], F32))
        crow = es2.enter_context(nc.sbuf_tensor(f"{kind}_crow", [1, ST],
                                                F32))
        scr = es2.enter_context(nc.sbuf_tensor(f"{kind}_scr", [128, 48],
                                               F32))
        inter = es2.enter_context(nc.sbuf_tensor(f"{kind}_int",
                                                 [128, EC * ST], BF16))
        wup = [es2.enter_context(nc.sbuf_tensor(f"{kind}_wu{i}",
                                                [128, 8192], BF16))
               for i in range(2)]
        wdn = [es2.enter_context(nc.sbuf_tensor(f"{kind}_wd{i}",
                                                [128, 4096], BF16))
               for i in range(2)]
        stt = es2.enter_context(nc.sbuf_tensor(f"{kind}_st", [128, 16],
                                               F32))
        yt = es2.enter_context(nc.sbuf_tensor(f"{kind}_yt", [128, 2 * D],
                                              F32))
        nt_ = es2.enter_context(nc.sbuf_tensor(f"{kind}_nt", [128, 2 * D],
                                               F32))
        ot = [es2.enter_context(nc.sbuf_tensor(f"{kind}_ot{i}", [128, D],
                                               F32)) for i in range(2)]
        psD = [es2.enter_context(nc.psum_tensor(f"{kind}_psD{i}", [128, D],
                                                F32)) for i in range(3)]
        psA = es2.enter_context(nc.psum_tensor(f"{kind}_psA", [128, 512],
                                               F32))
        pst1 = es2.enter_context(nc.psum_tensor(f"{kind}_pst",
                                                [128, 512], BF16))
        pst = [pst1, pst1]
        cscr = nc.dram_tensor(f"{kind}_cscr", [NE, ST], F32).ap()

        def G_ld(g):
            ds = DS["dg"]
            ds.inc(g.dma_start(
                out=gwt[:].rearrange("p (kc m) -> p kc m", m=16),
                in_=W["gwT"].rearrange("(kc p) m -> p kc m", p=128)),
                step=16)
            t = ds.inc(g.dma_start(out=gbr[:], in_=W["gb"].rearrange(
                "(o e) -> o e", o=1)), step=16)
            K.po.wait(t)
        K.G(G_ld)
        K.barrier()

        def G_bb(teng):
            t = K.te.inc(teng.matmul(psA[:, 0:16], K.ones_f32[:, 0:128],
                                     gbr[:], start=True, stop=True))
            K.te.wait(t)
        K.T(G_bb)

        def G_bb_v(v):
            K.ve.wait(K.te.tok())
            t = K.ve.inc(v.tensor_copy(gbb[:], psA[:, 0:16]))
            K.ve.wait(t)
            t = K.ve.inc(v.tensor_copy(gbrb[:], gbr[:]))
            K.ve.wait(t)
        K.V(G_bb_v)
        K.barrier()

        for sp in range(NST):
            tok0 = sp * ST
            ntok = min(ST, TOK - tok0)
            ntt = (ntok + 127) // 128
            p_cons = P["cons"]
            cons_b = p_cons.count
            p_evac = P["evac"]
            evac_b = p_evac.count
            dx = [DS["dx0"], DS["dx1"]]
            ld = []
            var_toks = []
            ap_toks = []
            te_b1 = K.te.count

            def A_ld(q, sp=sp, tok0=tok0, ntt=ntt, ld=ld, dx=dx,
                     cons_b=cons_b):
                for t in range(ntt):
                    if t >= 2:
                        K.sq.wait(p_cons.at(cons_b + (t - 2) + 1))
                    r0 = tok0 + t * 128
                    ld.append(dx[t % 2].inc(
                        q.dma_start(out=xt[t % 2][:],
                                    in_=x_in[r0:r0 + 128, :]), step=16))
            K.Q(A_ld)

            def A_ve(v, sp=sp, ntt=ntt, ld=ld, evac_b=evac_b,
                     te_b1=te_b1):
                for t in range(ntt + 1):
                    if t < ntt:
                        K.ve.wait(ld[t])
                        st8 = stt[:, (t % 2) * 8:(t % 2) * 8 + 8]
                        var_toks.append(ln_stats_v(K, v, xt[t % 2][:],
                                                   st8))
                    if t >= 1:
                        tt = t - 1
                        for half in range(2):
                            r = tt * 2 + half
                            K.ve.wait((K.te.sem, te_b1 + r + 1))
                            pb = pst[r % 2]
                            for j in range(4):
                                c = half * 4 + j
                                inst = v.tensor_copy(
                                    xfm[:, c * ST + tt * 128:
                                        c * ST + (tt + 1) * 128],
                                    pb[:, j * 128:(j + 1) * 128])
                                if j < 3:
                                    K.ve.inc(inst)
                            K.ve.inc(inst, psem=p_evac)
            K.V(A_ve)

            def A_s(s, sp=sp, ntt=ntt):
                for t in range(ntt):
                    K.se.wait(var_toks[t])
                    st8 = stt[:, (t % 2) * 8:(t % 2) * 8 + 8]
                    tA, _ = ln_apply_s(K, s, xt[t % 2][:],
                                       xl3[:, t * D:(t + 1) * D], st8,
                                       extra_psem=p_cons)
                    ap_toks.append(tA)
            K.Sx(A_s)

            def A_tr(teng, sp=sp, ntt=ntt, evac_b=evac_b):
                for t in range(ntt):
                    K.te.wait(ap_toks[t])
                    for half in range(2):
                        r = t * 2 + half
                        if r >= 1:
                            K.te.wait(p_evac.at(evac_b + r))
                        pb = pst[r % 2]
                        for j in range(4):
                            c = half * 4 + j
                            last = teng.transpose(
                                pb[:, j * 128:(j + 1) * 128],
                                xl3[:, t * D + c * 128:
                                    t * D + (c + 1) * 128],
                                K.ident[:])
                        K.te.inc(last)
            K.T(A_tr)

            p_rt = P["rt"]
            rt_b = p_rt.count
            p_cmb = P["cmb"]
            cmb_b = p_cmb.count
            p_cw = P["cw"]
            cw_b = p_cw.count
            rmm_toks = []

            def R_mm(teng, sp=sp, ntt=ntt, evac_b=evac_b, rt_b=rt_b):
                for t in range(ntt):
                    K.te.wait(p_evac.at(evac_b + 2 * t + 2))
                    if t >= 1:
                        K.te.wait(p_rt.at(rt_b + t))
                    for kc in range(DC):
                        teng.matmul(
                            psA[:, 0:16],
                            xfm[:, kc * ST + t * 128:kc * ST + (t + 1) * 128],
                            gwt[:, kc * 16:(kc + 1) * 16],
                            start=(kc == 0), stop=False)
                    if kind == "l2f":
                        last = teng.matmul(psA[:, 0:16],
                                           K.ones_bf[:, 0:128], gbrb[:],
                                           start=False, stop=True)
                    else:
                        last = teng.matmul(psA[:, 0:16],
                                           K.ones_bf[:, 0:128], K.zrow16_t[:],
                                           start=False, stop=True)
                    rmm_toks.append(K.te.inc(last))
            K.T(R_mm)

            def R_ev(s, sp=sp, ntt=ntt):
                for t in range(ntt):
                    K.se.wait(rmm_toks[t])
                    inst = s.activation(rt3[:, t * 16:(t + 1) * 16],
                                        psA[:, 0:16], AF.Copy, scale=1.0)
                    K.se.inc(inst, psem=p_rt)
            K.Sx(R_ev)

            def C_sig(s, sp=sp, ntt=ntt, rt_b=rt_b, cmb_b=cmb_b):
                for t in range(ntt):
                    K.se.wait(p_rt.at(rt_b + t + 1))
                    rr = rt3[:, t * 16:(t + 1) * 16]
                    inst = s.activation(rr, rr, AF.Sigmoid,
                                        bias=K.zero_ap, scale=1.0)
                    K.se.inc(inst, psem=p_cmb)
            K.Sx(C_sig)
            sig_base = p_cmb.count - (0 if kind != "l2f" else 0)

            def C_main(v, sp=sp, ntt=ntt, sig_base=sig_base):
                for t in range(ntt):
                    K.ve.wait(p_cmb.at(sig_base - ntt + t + 1))
                    K.ve.wait(K.ve.tok())
                    rr = rt3[:, t * 16:(t + 1) * 16]
                    cc = cmb[:, t * 16:(t + 1) * 16]
                    s8 = scr

                    def step(inst):
                        K.ve.inc(inst)
                        K.ve.wait(K.ve.tok())
                    if kind == "l2f":
                        step(v.reduce_max(s8[:, 0:1], rr, axis=AX.X))
                        step(v.tensor_scalar(s8[:, 8:24], rr, s8[:, 0:1],
                                             None, op0=ALU.is_ge))
                        step(v.scalar_tensor_tensor(
                            s8[:, 8:24], s8[:, 8:24], -1e30, rr,
                            op0=ALU.mult, op1=ALU.add))
                        step(v.reduce_max(s8[:, 1:2], s8[:, 8:24],
                                          axis=AX.X))
                        step(v.tensor_add(s8[:, 2:3], s8[:, 0:1],
                                          s8[:, 1:2]))
                        step(v.reciprocal(s8[:, 3:4], s8[:, 2:3]))
                        step(v.tensor_scalar(cc, rr, s8[:, 1:2], None,
                                             op0=ALU.is_ge))
                        step(v.tensor_mul(cc, cc, rr))
                        inst = v.tensor_scalar(cc, cc, s8[:, 3:4], None,
                                               op0=ALU.mult)
                        K.ve.inc(inst, psem=p_cw)
                    else:
                        step(v.tensor_add(rr, rr, gbb[:]))
                        scv = rr.rearrange("p (g e) -> p g e", e=4)
                        step(v.reduce_max(s8[:, 0:4], scv, axis=AX.X))
                        for i4 in range(4):
                            K.ve.inc(v.tensor_tensor(
                                cc.rearrange("p (g e) -> p g e",
                                             e=4)[:, :, i4:i4 + 1],
                                scv[:, :, i4:i4 + 1],
                                s8[:, 0:4].rearrange(
                                    "p (g e) -> p g e", e=1),
                                op=ALU.is_ge))
                        K.ve.wait(K.ve.tok())
                        step(v.scalar_tensor_tensor(
                            s8[:, 8:24], cc, -1e30, rr, op0=ALU.mult,
                            op1=ALU.add))
                        step(v.reduce_max(
                            s8[:, 4:8],
                            s8[:, 8:24].rearrange("p (g e) -> p g e",
                                                  e=4), axis=AX.X))
                        step(v.tensor_add(s8[:, 0:4], s8[:, 0:4],
                                          s8[:, 4:8]))
                        step(v.reduce_max(s8[:, 4:5], s8[:, 0:4],
                                          axis=AX.X))
                        step(v.tensor_scalar(s8[:, 8:12], s8[:, 0:4],
                                             s8[:, 4:5], None,
                                             op0=ALU.is_ge))
                        step(v.scalar_tensor_tensor(
                            s8[:, 12:16], s8[:, 8:12], -1e30, s8[:, 0:4],
                            op0=ALU.mult, op1=ALU.add))
                        step(v.reduce_max(s8[:, 5:6], s8[:, 12:16],
                                          axis=AX.X))
                        step(v.tensor_scalar(s8[:, 8:12], s8[:, 0:4],
                                             s8[:, 5:6], None,
                                             op0=ALU.is_ge))
                        for i4 in range(4):
                            K.ve.inc(v.tensor_copy(
                                cc.rearrange("p (g e) -> p g e",
                                             e=4)[:, :, i4:i4 + 1],
                                s8[:, 8:12].rearrange(
                                    "p (g e) -> p g e", e=1)))
                        K.ve.wait(K.ve.tok())
                        step(v.tensor_mul(cc, cc, rr))
                        step(v.tensor_relu(cc, cc))
                        step(v.tensor_copy(s8[:, 8:24], cc))
                        for it in range(4):
                            step(v.reduce_max(s8[:, 4:5], s8[:, 8:24],
                                              axis=AX.X))
                            if it < 3:
                                step(v.tensor_scalar(
                                    s8[:, 24:40], s8[:, 8:24],
                                    s8[:, 4:5], None, op0=ALU.is_ge))
                                step(v.scalar_tensor_tensor(
                                    s8[:, 8:24], s8[:, 24:40], -1e30,
                                    s8[:, 8:24], op0=ALU.mult,
                                    op1=ALU.add))
                        step(v.tensor_scalar(s8[:, 24:40], cc, s8[:, 4:5],
                                             None, op0=ALU.is_ge))
                        step(v.tensor_mul(s8[:, 24:40], s8[:, 24:40], cc))
                        step(v.reduce_sum(s8[:, 5:6], s8[:, 24:40],
                                          axis=AX.X))
                        step(v.tensor_scalar(s8[:, 5:6], s8[:, 5:6], 1.0,
                                             1e-20, op0=ALU.mult,
                                             op1=ALU.add))
                        step(v.reciprocal(s8[:, 6:7], s8[:, 5:6]))
                        inst = v.tensor_scalar(cc, s8[:, 24:40],
                                               s8[:, 6:7], 2.5,
                                               op0=ALU.mult, op1=ALU.mult)
                        K.ve.inc(inst, psem=p_cw)
            K.V(C_main)
            cmb_done = p_cw.count

            def C_wr(g, sp=sp, ntt=ntt, cmb_done=cmb_done):
                ds = DS["dcw"]
                K.po.wait(p_cw.at(cmb_done))
                for t in range(ntt):
                    t_ = ds.inc(g.dma_start(
                        out=cscr[0:16, t * 128:(t + 1) * 128].rearrange(
                            "e p -> p e"),
                        in_=cmb[:, t * 16:(t + 1) * 16],
                        allow_slow_non_contiguous=True), step=16)
                K.po.wait(t_)
            K.G(C_wr)
            cw_tok = (DS["dcw"].h, DS["dcw"].count)

            p_up = P["up"]
            p_ev2 = P["ev"]
            p_mul = P["mul"]
            p_bc = P["bc"]
            p_dn = P["dn"]
            p_epi = P["epi"]
            epi_b = p_epi.count
            p_psa = P["psa"]
            dup = [DS["dup0"], DS["dup1"]]
            dwd = [DS["dwd0"], DS["dwd1"]]
            dcr = DS["dcr"]
            evac_full = (p_evac.h, evac_b + 2 * ntt)

            for e in range(NE):
                up_b = p_up.count
                ev_b = p_ev2.count
                mul_b = p_mul.count
                bc_b = p_bc.count
                dn_b = p_dn.count
                dup_b = dup[e % 2].count
                dwd_b = dwd[e % 2].count
                dcr_b = dcr.count

                def E_ldup(q, e=e, dup_b=dup_b):
                    if e >= 2:
                        K.sq.wait(p_ev2.at(ev_b - (EC * 2) + EC * 2))
                        K.sq.wait(p_mul.at(mul_b))
                    r0 = e * 8192
                    dup[e % 2].inc(q.dma_start(
                        out=wup[e % 2][:].rearrange("p (c m) -> p c m",
                                                    m=128),
                        in_=W["WUP"][r0:r0 + 8192, :].rearrange(
                            "(c p) m -> p c m", p=128)), step=16)
                K.Q(E_ldup)

                def E_ldcr(g, e=e, dcr_b=dcr_b, cw_tok=cw_tok):
                    if e < E:
                        K.po.wait(cw_tok)
                        K.po.wait(p_bc.at(bc_b))
                        dcr.inc(g.dma_start(out=crow[:, 0:ST],
                                            in_=cscr[e:e + 1, :]),
                                step=16)
                K.G(E_ldcr)

                up_toks = []

                def E_up(teng, e=e, ntt=ntt, dup_b=dup_b, up_b=up_b,
                         ev_b=ev_b, dcr_b=dcr_b):
                    K.te.wait(dup[e % 2].at(dup_b + 16))
                    K.te.wait(evac_full)
                    K.te.wait(p_rt.at(p_rt.count))
                    K.te.wait(p_bc.at(bc_b))
                    for u in range(2):
                        for mc in range(EC):
                            r = u * EC + mc
                            if True:
                                K.te.wait(p_ev2.at(ev_b + r))
                            for kc in range(DC):
                                last = teng.matmul(
                                    psA[:, 0:ST],
                                    wup[e % 2][:, (u * EC + mc) * 8 * 128
                                               + kc * 128:
                                               (u * EC + mc) * 8 * 128
                                               + (kc + 1) * 128],
                                    xfm[:, kc * ST:kc * ST + ST],
                                    start=(kc == 0), stop=(kc == DC - 1))
                            up_toks.append(K.te.inc(last))
                    if e < E:
                        K.te.wait(dcr.at(dcr_b + 16))
                        K.te.wait(p_mul.at(mul_b + EC))
                        inst = teng.matmul(psA[:, 0:ST],
                                           K.ones_f32[:, 0:128],
                                           crow[:, 0:ST], start=True,
                                           stop=True)
                        K.te.inc(inst, psem=p_bc)
                K.T(E_up)

                def E_ev(s, e=e, up_b=up_b):
                    for u in range(2):
                        for mc in range(EC):
                            r = u * EC + mc
                            K.se.wait(up_toks[r])
                            if u == 0:
                                inst = s.activation(
                                    inter[:, mc * ST:(mc + 1) * ST],
                                    psA[:, 0:ST], AF.Silu,
                                    bias=K.zero_ap, scale=1.0)
                            else:
                                inst = s.activation(
                                    yt[:].bitcast(BF16)[:, mc * ST:
                                                        (mc + 1) * ST],
                                    psA[:, 0:ST], AF.Identity,
                                    bias=K.zero_ap, scale=1.0)
                            K.se.inc(inst, psem=p_ev2)
                K.Sx(E_ev)

                def E_mul(v, e=e, ev_b=ev_b):
                    for mc in range(EC):
                        K.ve.wait(p_ev2.at(ev_b + EC + mc + 1))
                        K.ve.wait(p_ev2.at(ev_b + mc + 1))
                        inst = v.tensor_mul(
                            inter[:, mc * ST:(mc + 1) * ST],
                            inter[:, mc * ST:(mc + 1) * ST],
                            yt[:].bitcast(BF16)[:, mc * ST:(mc + 1) * ST])
                        K.ve.inc(inst, psem=p_mul)
                K.V(E_mul)

                def E_scale(v, e=e, bc_b=bc_b, mul_b=mul_b):
                    if e >= E:
                        return
                    K.ve.wait(p_bc.at(bc_b + 1))
                    for mc in range(EC):
                        K.ve.wait(p_mul.at(mul_b + EC))
                        inst = v.tensor_tensor(
                            inter[:, mc * ST:(mc + 1) * ST],
                            inter[:, mc * ST:(mc + 1) * ST],
                            psA[:, 0:ST], op=ALU.mult)
                        K.ve.inc(inst, psem=p_bc)
                K.V(E_scale)
                int_tok = ((p_bc.h, p_bc.count) if e < E
                           else (p_mul.h, p_mul.count))

                def E_lddn(q, e=e, dwd_b=dwd_b, dn_b=dn_b):
                    if e >= 2:
                        K.sq.wait(p_dn.at(dn_b))
                    r0 = e * 512
                    dwd[e % 2].inc(q.dma_start(
                        out=wdn[e % 2][:].rearrange("p (c m) -> p c m",
                                                    m=1024),
                        in_=W["WDN"][r0:r0 + 512, :].rearrange(
                            "(c p) m -> p c m", p=128)), step=16)
                K.Q(E_lddn)

                def E_dn(teng, e=e, ntt=ntt, dwd_b=dwd_b, int_tok=int_tok,
                         epi_b=epi_b):
                    K.te.wait(dwd[e % 2].at(dwd_b + 16))
                    K.te.wait(int_tok)
                    if e == 0:
                        K.te.wait(p_epi.at(epi_b))
                    for t in range(ntt):
                        for half in range(2):
                            for kc in range(EC):
                                last = teng.matmul(
                                    psD[t][:, half * 512:(half + 1) * 512],
                                    inter[:, kc * ST + t * 128:
                                          kc * ST + (t + 1) * 128],
                                    wdn[e % 2][:, kc * 1024 + half * 512:
                                               kc * 1024 + (half + 1) * 512],
                                    start=(e == 0 and kc == 0),
                                    stop=(e == NE - 1 and kc == EC - 1))
                    K.te.inc(last, psem=p_dn)
                K.T(E_dn)

            p_var = P["var"]
            var_b = p_var.count
            p_rstd = P["rstd"]
            rstd_b = p_rstd.count
            p_app = P["app"]
            app_b = p_app.count
            d2 = [DS["d20"], DS["d21"]]
            dot = [DS["dot0"], DS["dot1"]]
            dot_b = [d.count for d in dot]
            dn_done = (p_dn.h, p_dn.count)
            ld2 = []

            def EP_ld2(q, sp=sp, tok0=tok0, ntt=ntt, ld2=ld2, d2=d2,
                       app_b=app_b):
                for t in range(ntt):
                    if t >= 2:
                        K.sq.wait(p_app.at(app_b + (t - 2) + 1))
                    r0 = tok0 + t * 128
                    ld2.append(d2[t % 2].inc(q.dma_start(
                        out=x2[t % 2][:], in_=x_in[r0:r0 + 128, :]),
                        step=16))
            K.Q(EP_ld2)

            def EP_dve(v, sp=sp, ntt=ntt, ld2=ld2, dn_done=dn_done,
                       dot_b=dot_b):
                for t in range(ntt):
                    sl = t % 2
                    st8 = stt[:, sl * 8:sl * 8 + 8]
                    ytile = yt[:, sl * D:sl * D + D]
                    K.ve.wait(dn_done)
                    K.ve.wait(K.ve.tok())
                    if kind == "l2f":
                        inst = v.scalar_tensor_tensor(
                            ytile, psD[t][:], 1.0,
                            xl3[:, t * D:(t + 1) * D], op0=ALU.mult,
                            op1=ALU.add, accum_out=st8[:, 0:1])
                    else:
                        inst = v.tensor_scalar(
                            ytile, psD[t][:], 1.0, 0.0, op0=ALU.mult,
                            op1=ALU.add, accum_out=st8[:, 0:1])
                    K.ve.inc(inst, psem=p_epi)
                    K.ve.wait(p_epi.at(p_epi.count))
                    t2 = K.ve.inc(v.tensor_scalar(st8[:, 1:2],
                                                  st8[:, 0:1], 1.0 / D,
                                                  None, op0=ALU.mult))
                    t3 = K.ve.inc(v.scalar_tensor_tensor(
                        K.sqscr[:, 0:D], ytile, 1.0, ytile, op0=ALU.mult,
                        op1=ALU.mult, accum_out=st8[:, 2:3]))
                    K.ve.wait(t2, t3)
                    t4 = K.ve.inc(v.tensor_scalar(st8[:, 5:6],
                                                  st8[:, 1:2],
                                                  st8[:, 1:2], None,
                                                  op0=ALU.mult))
                    K.ve.wait(t4)
                    inst = v.tensor_scalar(st8[:, 3:4], st8[:, 2:3],
                                           1.0 / D, st8[:, 5:6],
                                           op0=ALU.mult,
                                           op1=ALU.subtract)
                    K.ve.inc(inst, psem=p_var)
                    K.ve.wait(p_rstd.at(rstd_b + t + 1))
                    ntile = nt_[:, sl * D:sl * D + D]
                    t5 = K.ve.inc(v.tensor_scalar(ntile, ytile,
                                                  st8[:, 1:2],
                                                  st8[:, 4:5],
                                                  op0=ALU.subtract,
                                                  op1=ALU.mult))
                    K.ve.wait(t5, ld2[t])
                    if t >= 2:
                        K.ve.wait(dot[t % 2].at(dot_b[t % 2]
                                                + 16 * (t // 2)))
                    inst = v.tensor_add(ot[t % 2][:], ntile,
                                        x2[t % 2][:])
                    K.ve.inc(inst, psem=p_app)
            K.V(EP_dve)

            def EP_act(s, sp=sp, ntt=ntt, var_b=var_b):
                for t in range(ntt):
                    sl = t % 2
                    st8 = stt[:, sl * 8:sl * 8 + 8]
                    K.se.wait(p_var.at(var_b + t + 1))
                    t6 = K.se.inc(s.activation(st8[:, 4:5], st8[:, 3:4],
                                               AF.Ln, bias=K.eps_ap,
                                               scale=1.0))
                    K.se.wait(t6)
                    inst = s.activation(st8[:, 4:5], st8[:, 4:5], AF.Exp,
                                        bias=K.zero_ap, scale=-0.5)
                    K.se.inc(inst, psem=p_rstd)
            K.Sx(EP_act)

            def EP_st(g, sp=sp, tok0=tok0, ntt=ntt, dot=dot, app_b=app_b):
                for t in range(ntt):
                    K.po.wait(p_app.at(app_b + t + 1))
                    r0 = tok0 + t * 128
                    tk = dot[t % 2].inc(
                        g.dma_start(out=x_out[r0:r0 + 128, :],
                                    in_=ot[t % 2][:]), step=16)
                K.po.wait(tk)
            K.G(EP_st)
            K.barrier()



def emit_l1_attn(K, es, x_in, x_out, W, upto=99):
    from concourse.ap import AP
    nc = K.nc
    NH = 16
    NST = TOK // 128
    with ExitStack() as es2:
        wq = es2.enter_context(nc.sbuf_tensor("a1_wq", [128, 64 * 128],
                                              BF16))
        wk = es2.enter_context(nc.sbuf_tensor("a1_wk", [128, 64 * 128],
                                              BF16))
        wv = es2.enter_context(nc.sbuf_tensor("a1_wv", [128, 8 * 1024],
                                              BF16))
        wo = es2.enter_context(nc.sbuf_tensor("a1_wo", [128, 8 * 1024],
                                              BF16))
        tk2 = es2.enter_context(nc.sbuf_tensor("a1_tk", [128, 32], BF16))
        rvb = es2.enter_context(nc.sbuf_tensor("a1_rv", [31, 64], BF16))
        msk = es2.enter_context(nc.sbuf_tensor("a1_msk", [128, 2048], BF16))
        bqs = es2.enter_context(nc.sbuf_tensor("a1_bqs", [128, 8], F32))
        bkc = es2.enter_context(nc.sbuf_tensor("a1_bk", [128, 8], F32))
        bvr = es2.enter_context(nc.sbuf_tensor("a1_bvr", [1, 1024], BF16))
        bor = es2.enter_context(nc.sbuf_tensor("a1_bor", [1, 1024], BF16))
        stg = es2.enter_context(nc.sbuf_tensor("a1_stg", [128, 64], F32))
        stg2 = es2.enter_context(nc.sbuf_tensor("a1_stg2", [1, 2048], F32))
        xt = es2.enter_context(nc.sbuf_tensor("a1_xt", [128, D], F32))
        xl = es2.enter_context(nc.sbuf_tensor("a1_xl", [128, D], BF16))
        xfm = es2.enter_context(nc.sbuf_tensor("a1_xfm", [128, D], BF16))
        qfm = es2.enter_context(nc.sbuf_tensor("a1_qfm", [128, D], BF16))
        kfm = es2.enter_context(nc.sbuf_tensor("a1_kfm", [128, D], BF16))
        qhd = es2.enter_context(nc.sbuf_tensor("a1_qhd", [64, 2048], BF16))
        khd = es2.enter_context(nc.sbuf_tensor("a1_khd", [64, 2048], BF16))
        vtm = es2.enter_context(nc.sbuf_tensor("a1_vtm", [128, D], BF16))
        srx = es2.enter_context(nc.sbuf_tensor("a1_srx", [128, 512], F32))
        xcomp = es2.enter_context(nc.sbuf_tensor("a1_xc", [128, 256], F32))
        Pf = es2.enter_context(nc.sbuf_tensor("a1_pf", [128, 2048], F32))
        xrt = es2.enter_context(nc.sbuf_tensor("a1_xrt", [128, 2048], F32))
        P2c = es2.enter_context(nc.sbuf_tensor("a1_p2c", [128, 256], F32))
        ztc = es2.enter_context(nc.sbuf_tensor("a1_ztc", [128, 256], F32))
        MM = es2.enter_context(nc.sbuf_tensor("a1_mm", [128, 64], F32))
        Pb = es2.enter_context(nc.sbuf_tensor("a1_pb", [128, 2048], BF16))
        Pcn = es2.enter_context(nc.sbuf_tensor("a1_pcn", [128, 256], BF16))
        Pd = es2.enter_context(nc.sbuf_tensor("a1_pd", [128, 496], BF16))
        PbT = es2.enter_context(nc.sbuf_tensor("a1_pbt", [128, 128], BF16))
        PdT = es2.enter_context(nc.sbuf_tensor("a1_pdt", [31, 128], BF16))
        afm = es2.enter_context(nc.sbuf_tensor("a1_afm", [128, D], BF16))
        yt = es2.enter_context(nc.sbuf_tensor("a1_yt", [128, D], F32))
        stt = es2.enter_context(nc.sbuf_tensor("a1_st", [128, 8], F32))
        psA = [es2.enter_context(nc.psum_tensor(f"a1_psA{i}", [128, 512],
                                                F32)) for i in range(2)]
        psC = [es2.enter_context(nc.psum_tensor(f"a1_psC{i}", [128, 512],
                                                F32)) for i in range(2)]
        psB = [es2.enter_context(nc.psum_tensor(f"a1_psB{i}", [128, 512],
                                                F32)) for i in range(2)]
        psR = es2.enter_context(nc.psum_tensor("a1_psR", [128, 512], F32))
        pst = es2.enter_context(nc.psum_tensor("a1_pst", [128, 512], BF16))
        psS = [psA[0], psA[1], psC[0], psC[1]]

        DS = K.named_psems(es2, "a1d", ["dw", "dx", "dex", "dst"])
        P = K.named_psems(es2, "a1", ["tr", "ev", "mm"])

        def L_w(q):
            ds = DS["dw"]
            ds.inc(q.dma_start(
                out=wq[:].rearrange("p (m k n) -> p m k n", m=8, k=8),
                in_=W["wqb"].rearrange("(m k p) n -> p m k n", m=8, k=8)),
                step=16)
            ds.inc(q.dma_start(
                out=wk[:].rearrange("p (m k n) -> p m k n", m=8, k=8),
                in_=W["wkb"].rearrange("(m k p) n -> p m k n", m=8, k=8)),
                step=16)
            ds.inc(q.dma_start(
                out=wv[:].rearrange("p (k n) -> p k n", k=8),
                in_=W["wvb"].rearrange("(k p) n -> p k n", k=8)), step=16)
            ds.inc(q.dma_start(
                out=wo[:].rearrange("p (k n) -> p k n", k=8),
                in_=W["wob"].rearrange("(k p) n -> p k n", k=8)), step=16)
            ds.inc(q.dma_start(out=tk2[:], in_=W["tkb"]), step=16)
            ds.inc(q.dma_start(out=rvb[:], in_=W["rvb"][0:31, :]), step=16)
            ds.inc(q.dma_start(out=msk[:], in_=W["msk"]), step=16)
            ds.inc(q.dma_start(out=stg[:, 0:8], in_=W["bq"]), step=16)
            ds.inc(q.dma_start(out=stg[:, 8:16], in_=W["bk"]), step=16)
            ds.inc(q.dma_start(out=stg2[0:1, 0:1024],
                               in_=W["bv"].rearrange(
                "(o d) -> o d", o=1)), step=16)
            t = ds.inc(q.dma_start(out=stg2[0:1, 1024:2048],
                                   in_=W["bo"].rearrange(
                "(o d) -> o d", o=1)), step=16)
            K.sq.wait(t)
        K.Q(L_w)
        K.barrier()

        def L_prep(v):
            K.ve.inc(v.tensor_scalar(bqs[:], stg[:, 0:8], 0.125, None,
                                     op0=ALU.mult))
            K.ve.inc(v.tensor_copy(bkc[:], stg[:, 8:16]))
            K.ve.inc(v.tensor_copy(bvr[:], stg2[0:1, 0:1024]))
            t = K.ve.inc(v.tensor_copy(bor[:], stg2[0:1, 1024:2048]))
            K.ve.wait(t)
        K.V(L_prep)
        K.barrier()

        FSsr, FSxc, FSpf, FSpd = 512, 256, 2048, 496

        for i in range(NST):
            r0 = i * 128

            if upto < 1:
                continue
            def A_ld(q, r0=r0):
                t = DS["dx"].inc(q.dma_start(out=xt[:],
                                             in_=x_in[r0:r0 + 128, :]),
                                 step=16)
                K.sq.wait(t)
            K.Q(A_ld)
            K.barrier()

            if upto < 2:
                continue
            vt = []

            def B_st(v, vt=vt):
                vt.append(ln_stats_v(K, v, xt[:], stt[:, 0:8]))
            K.V(B_st)

            def B_ap(s, vt=vt):
                K.se.wait(vt[0])
                ln_apply_s(K, s, xt[:], xl[:], stt[:, 0:8])
            K.Sx(B_ap)
            K.barrier()

            if upto < 3:
                continue
            for wave in range(2):
                def C_tr(t, wave=wave):
                    for j in range(4):
                        c = wave * 4 + j
                        last = t.transpose(pst[:, j * 128:(j + 1) * 128],
                                           xl[:, c * 128:(c + 1) * 128],
                                           K.ident[:])
                    K.te.inc(last)
                K.T(C_tr)
                K.barrier()

                def C_ev(v, wave=wave):
                    for j in range(4):
                        c = wave * 4 + j
                        inst = v.tensor_copy(xfm[:, c * 128:(c + 1) * 128],
                                             pst[:, j * 128:(j + 1) * 128])
                    K.ve.inc(inst)
                K.V(C_ev)
                K.barrier()

            if upto < 4:
                continue
            def E_mm(t):
                for mc in range(8):
                    tgt = psA[mc // 4]
                    col = (mc % 4) * 128
                    for kc in range(8):
                        t.matmul(tgt[:, col:col + 128],
                                 wq[:, (mc * 8 + kc) * 128:
                                    (mc * 8 + kc + 1) * 128],
                                 xfm[:, kc * 128:(kc + 1) * 128],
                                 start=(kc == 0), stop=(kc == 7))
                for mc in range(8):
                    tgt = psC[mc // 4]
                    col = (mc % 4) * 128
                    for kc in range(8):
                        t.matmul(tgt[:, col:col + 128],
                                 wk[:, (mc * 8 + kc) * 128:
                                    (mc * 8 + kc + 1) * 128],
                                 xfm[:, kc * 128:(kc + 1) * 128],
                                 start=(kc == 0), stop=(kc == 7))
                for half in range(2):
                    tgt = psB[half]
                    for kc in range(8):
                        t.matmul(tgt[:],
                                 xfm[:, kc * 128:(kc + 1) * 128],
                                 wv[:, kc * 1024 + half * 512:
                                    kc * 1024 + half * 512 + 512],
                                 start=(kc == 0), stop=False)
                    inst = t.matmul(tgt[:], K.ones_bf[0:1, 0:128],
                                    bvr[0:1, half * 512:half * 512 + 512],
                                    start=False, stop=True)
                K.te.inc(inst)
            K.T(E_mm)
            K.barrier()

            if upto < 5:
                continue
            def F_evq(s):
                for mc in range(8):
                    pa = psA[mc // 4][:, (mc % 4) * 128:(mc % 4) * 128 + 128]
                    inst = s.activation(qfm[:, mc * 128:(mc + 1) * 128], pa,
                                        AF.Identity,
                                        bias=bqs[:, mc:mc + 1], scale=0.125)
                    K.se.inc(inst)
                for mc in range(8):
                    pc = psC[mc // 4][:, (mc % 4) * 128:(mc % 4) * 128 + 128]
                    inst = s.activation(kfm[:, mc * 128:(mc + 1) * 128], pc,
                                        AF.Identity,
                                        bias=bkc[:, mc:mc + 1], scale=1.0)
                    K.se.inc(inst)
            K.Sx(F_evq)

            def F_evv(v):
                K.ve.inc(v.tensor_copy(vtm[:, 0:512], psB[0][:]))
                K.ve.inc(v.tensor_copy(vtm[:, 512:1024], psB[1][:]))
            K.V(F_evv)
            K.barrier()

            def F2_dma(q):
                ds = DS["dex"]
                for (srcb, dstb) in ((qfm, qhd), (kfm, khd)):
                    for par in range(2):
                        src = AP(tensor=srcb[:].tensor, offset=par * 64 * D,
                                 ap=[[D, 64], [1, 1024]])
                        dst = AP(tensor=dstb[:].tensor, offset=par * 128,
                                 ap=[[2048, 64], [256, 8], [1, 128]])
                        t = ds.inc(q.dma_start(out=dst, in_=src), step=16)
                K.sq.wait(t)
            K.Q(F2_dma)
            K.barrier()

            if upto < 6:
                continue
            def G_mm(t):
                for h in range(NH):
                    qh = qhd[0:64, h * 128:(h + 1) * 128]
                    kh = khd[0:64, h * 128:(h + 1) * 128]
                    ps = psS[h // 4]
                    col = (h % 4) * 128
                    t.matmul(ps[:, col:col + 128], qh, kh,
                             start=True, stop=True)
                    inst = t.matmul(psR[:, h * 32:(h + 1) * 32], qh,
                                    tk2[0:64, :],
                                    start=True, stop=True)
                K.te.inc(inst)
            K.T(G_mm)
            K.barrier()

            if upto < 7:
                continue
            def H_v(v):
                for g in range(4):
                    K.ve.inc(v.tensor_reduce(
                        MM[:, g * 4:g * 4 + 4],
                        psS[g][:].rearrange("p (h k) -> p h k", h=4),
                        op=ALU.max, axis=AX.X))
                t = K.ve.inc(v.tensor_copy(srx[:], psR[:]))
                K.ve.wait(t)
                t = K.ve.inc(v.tensor_reduce(
                    MM[:, 16:32],
                    srx[:].rearrange("p (h k) -> p h k", h=NH),
                    op=ALU.max, axis=AX.X))
                K.ve.wait(t)
                for h in range(NH):
                    inst = v.tensor_scalar(srx[:, h * 32:(h + 1) * 32],
                                           srx[:, h * 32:(h + 1) * 32],
                                           MM[:, 16 + h:17 + h], None,
                                           op0=ALU.subtract)
                    K.ve.inc(inst)
            K.V(H_v)

            def H_s(s):
                K.se.wait(K.ve.tok())
                t = K.se.inc(s.activation(srx[:], srx[:], AF.Exp,
                                          bias=K.zero_ap, scale=1.0))
                K.se.wait(t)
            K.Sx(H_s)
            K.barrier()

            if upto < 8:
                continue
            def I_dma(q):
                ds = DS["dex"]
                for qi in range(16):
                    src = AP(tensor=srx[:].tensor,
                             offset=qi * FSsr + 15 - qi,
                             ap=[[16 * FSsr, 8], [32, 16], [1, 16]])
                    dst = AP(tensor=xcomp[:].tensor, offset=qi * FSxc,
                             ap=[[16 * FSxc, 8], [16, 16], [1, 16]])
                    t = ds.inc(q.dma_start(out=dst, in_=src), step=16)
                K.sq.wait(t)
            K.Q(I_dma)
            K.barrier()

            if upto < 9:
                continue
            def J_sub(v):
                for h in range(NH):
                    ps = psS[h // 4]
                    col = (h % 4) * 128
                    inst = v.tensor_scalar(Pf[:, h * 128:(h + 1) * 128],
                                           ps[:, col:col + 128],
                                           MM[:, h:h + 1], None,
                                           op0=ALU.subtract)
                    K.ve.inc(inst)
            K.V(J_sub)

            def J_exp(s):
                K.se.wait(K.ve.tok())
                t = K.se.inc(s.activation(Pf[:], Pf[:], AF.Exp,
                                          bias=K.zero_ap, scale=1.0))
                K.se.wait(t)
            K.Sx(J_exp)

            def J_msk(v):
                K.ve.wait(K.se.tok())
                t = K.ve.inc(v.tensor_tensor(Pf[:], Pf[:], msk[:],
                                             op=ALU.mult))
                K.ve.wait(t)
            K.V(J_msk)
            K.barrier()

            if upto < 10:
                continue
            def K_dma(q):
                ds = DS["dex"]
                for c in range(8):
                    src = AP(tensor=Pf[:].tensor,
                             offset=16 * c * FSpf + 16 * c,
                             ap=[[FSpf, 16], [128, 16], [1, 16]])
                    dst = AP(tensor=P2c[:].tensor, offset=16 * c * FSxc,
                             ap=[[FSxc, 16], [16, 16], [1, 16]])
                    ds.inc(q.dma_start(out=dst, in_=src), step=16)
                for cc in range(8):
                    src = AP(tensor=xcomp[:].tensor, offset=0,
                             ap=[[FSxc, 128], [16, 16], [1, 16]])
                    dst = AP(tensor=xrt[:].tensor, offset=cc * 16,
                             ap=[[FSpf, 128], [128, 16], [1, 16]])
                    t = ds.inc(q.dma_start(out=dst, in_=src), step=16)
                K.sq.wait(t)
            K.Q(K_dma)
            K.barrier()

            if upto < 11:
                continue
            def L_v(v):
                t1 = K.ve.inc(v.tensor_tensor(ztc[:], P2c[:], xcomp[:],
                                              op=ALU.mult))
                K.ve.wait(t1)
                t2 = K.ve.inc(v.tensor_reduce(
                    MM[:, 32:48],
                    ztc[:].rearrange("p (h k) -> p h k", h=NH),
                    op=ALU.add, axis=AX.X))
                K.ve.wait(t2)
                t3 = K.ve.inc(v.reciprocal(MM[:, 48:64], MM[:, 32:48]))
                t4 = K.ve.inc(v.tensor_tensor(Pf[:], Pf[:], xrt[:],
                                              op=ALU.mult))
                K.ve.wait(t3, t4)
                for h in range(NH):
                    K.ve.inc(v.tensor_scalar(Pb[:, h * 128:(h + 1) * 128],
                                             Pf[:, h * 128:(h + 1) * 128],
                                             MM[:, 48 + h:49 + h], None,
                                             op0=ALU.mult))
                    K.ve.inc(v.tensor_scalar(Pcn[:, h * 16:(h + 1) * 16],
                                             ztc[:, h * 16:(h + 1) * 16],
                                             MM[:, 48 + h:49 + h], None,
                                             op0=ALU.mult))
                t5 = K.ve.inc(v.memset(Pd[:], 0.0))
                K.ve.wait(t5)
            K.V(L_v)
            K.barrier()

            if upto < 12:
                continue
            def M_dma(q):
                ds = DS["dex"]
                for qi in range(16):
                    src = AP(tensor=Pcn[:].tensor, offset=qi * FSxc,
                             ap=[[16 * FSxc, 8], [16, 16], [1, 16]])
                    dst = AP(tensor=Pd[:].tensor,
                             offset=qi * FSpd + 15 - qi,
                             ap=[[16 * FSpd, 8], [31, 16], [1, 16]])
                    t = ds.inc(q.dma_start(out=dst, in_=src), step=16)
                K.sq.wait(t)
            K.Q(M_dma)
            K.barrier()

            if upto < 13:
                continue
            p_tr, p_ev, p_mm = P["tr"], P["ev"], P["mm"]
            tr_b, ev_b, mm_b = p_tr.count, p_ev.count, p_mm.count

            def N_t(t):
                for h in range(NH):
                    if h > 0:
                        K.te.wait(p_ev.at(ev_b + h))
                    t.transpose(pst[:, 0:128],
                                Pb[:, h * 128:(h + 1) * 128], K.ident[:])
                    inst = t.transpose(pst[0:31, 128:256],
                                       Pd[:, h * 31:(h + 1) * 31],
                                       K.ident[:])
                    p_tr.inc(inst)
                    K.te.wait(p_ev.at(ev_b + h + 1))
                    tgt = psB[h // 8]
                    sl = tgt[(h % 2) * 64:(h % 2) * 64 + 64,
                             ((h // 2) % 4) * 128:((h // 2) % 4) * 128 + 128]
                    t.matmul(sl, vtm[:, h * 64:(h + 1) * 64], PbT[:],
                             start=True, stop=False)
                    inst = t.matmul(sl, rvb[:], PdT[:],
                                    start=False, stop=True)
                    p_mm.inc(inst)
            K.T(N_t)

            def N_v(v):
                for h in range(NH):
                    K.ve.wait(p_tr.at(tr_b + h + 1))
                    if h > 0:
                        K.ve.wait(p_mm.at(mm_b + h))
                    v.tensor_copy(PbT[:], pst[:, 0:128])
                    inst = v.tensor_copy(PdT[:], pst[0:31, 128:256])
                    p_ev.inc(inst)
            K.V(N_v)
            K.barrier()

            def N_ev(s):
                K.se.inc(s.activation(afm[:, 0:512], psB[0][:], AF.Identity,
                                      bias=K.zero_ap, scale=1.0))
                t = K.se.inc(s.activation(afm[:, 512:1024], psB[1][:],
                                          AF.Identity, bias=K.zero_ap,
                                          scale=1.0))
                K.se.wait(t)
            K.Sx(N_ev)
            K.barrier()

            if upto < 15:
                continue
            def O_mm(t):
                for half in range(2):
                    tgt = psA[half]
                    for kc in range(8):
                        t.matmul(tgt[:],
                                 afm[:, kc * 128:(kc + 1) * 128],
                                 wo[:, kc * 1024 + half * 512:
                                    kc * 1024 + half * 512 + 512],
                                 start=(kc == 0), stop=False)
                    inst = t.matmul(tgt[:], K.ones_bf[0:1, 0:128],
                                    bor[0:1, half * 512:half * 512 + 512],
                                    start=False, stop=True)
                K.te.inc(inst)
            K.T(O_mm)
            K.barrier()

            if upto < 16:
                continue
            def P_ev(s):
                K.se.inc(s.activation(yt[:, 0:512], psA[0][:], AF.Identity,
                                      bias=K.zero_ap, scale=1.0))
                t = K.se.inc(s.activation(yt[:, 512:1024], psA[1][:],
                                          AF.Identity, bias=K.zero_ap,
                                          scale=1.0))
                K.se.wait(t)
            K.Sx(P_ev)
            K.barrier()

            vt2 = []

            def P_st(v, vt2=vt2):
                vt2.append(ln_stats_v(K, v, yt[:], stt[:, 0:8]))
            K.V(P_st)

            def P_ap(s, vt2=vt2):
                K.se.wait(vt2[0])
                ln_apply_s(K, s, yt[:], yt[:], stt[:, 0:8])
            K.Sx(P_ap)
            K.barrier()

            def P_add(v):
                t = K.ve.inc(v.tensor_tensor(yt[:], yt[:], xt[:],
                                             op=ALU.add))
                K.ve.wait(t)
            K.V(P_add)
            K.barrier()

            if upto < 17:
                continue
            def Q_st(q, r0=r0):
                t = DS["dst"].inc(q.dma_start(out=x_out[r0:r0 + 128, :],
                                              in_=yt[:]), step=16)
                K.sq.wait(t)
            K.Q(Q_st)
            K.barrier()



def _const_blob():
    c = np.zeros((130, 128), dtype=np.float32)
    c[0:128, 0:128] = np.eye(128, dtype=np.float32)
    c[128, :] = 1.0
    return c


_BLOCK_EMITTERS = {}


def build_single_block_nc(name):
    nc = bass.Bass("TRN2", target_bir_lowering=False, debug=False)
    x_in = nc.dram_tensor("x_in", [TOK, D], F32, kind="ExternalInput").ap()
    cst = nc.dram_tensor("cst", [130, 128], F32, kind="ExternalInput").ap()
    x_out = nc.dram_tensor("x_out", [TOK, D], F32, kind="ExternalOutput").ap()
    with ExitStack() as es:
        block = es.enter_context(nc.Block())
        K = KCtx(nc, block)
        K.attach(es)
        emit_commons(K, es, cst)
        if name == "l0_attn":
            emit_l0_attn(K, es, x_in, x_out)
        elif name == "l0_ffn":
            blobs = {}
            jobs = []
            shapes = {"w1": (32, 8), "w2": (32, 8), "h1": (32, 32),
                      "h2": (32, 32), "w3": (8, 32)}
            for nm, (MC, KC) in shapes.items():
                R = MC * KC * 128
                f = nc.dram_tensor(nm + "f", [R, 128], F32,
                                   kind="ExternalInput").ap()
                bq = nc.dram_tensor(nm + "b", [R, 128], BF16).ap()
                jobs.append((f.rearrange("(a b) m -> a (b m)", b=8),
                             bq.rearrange("(a b) m -> a (b m)", b=8)))
                blobs[nm + "b"] = bq
            for nm, n in [("s1", 4096), ("s2", 4096), ("sh1", 4096),
                          ("sh2", 4096)]:
                blobs[nm] = nc.dram_tensor(nm, [128, n // 128], F32,
                                           kind="ExternalInput").ap()
            blobs["s3"] = nc.dram_tensor("s3", [1024], F32,
                                         kind="ExternalInput").ap()
            emit_weight_prep(K, es, jobs)
            emit_l0_ffn(K, es, x_in, x_out, blobs)
        elif name == "l1_ffn":
            blobs = {}
            jobs = []
            shapes = {"wg": (24, 8), "wu": (24, 8), "wd": (8, 24)}
            for nm, (MC, KC) in shapes.items():
                R = MC * KC * 128
                f = nc.dram_tensor(nm + "f", [R, 128], I32,
                                   kind="ExternalInput").ap()
                bq = nc.dram_tensor(nm + "b", [R, 128], BF16).ap()
                jobs.append((f.rearrange("(a b) m -> a (b m)", b=8),
                             bq.rearrange("(a b) m -> a (b m)", b=8)))
                blobs[nm + "b"] = bq
            emit_weight_prep(K, es, jobs)
            emit_l0_ffn(K, es, x_in, x_out, blobs, kind="l1")
        elif name == "l1_attn":
            blobs = {}
            jobs = []
            shapes = {"wq": (8192, 128), "wk": (8192, 128),
                      "wv": (1024, 1024), "wo": (1024, 1024),
                      "tk": (128, 32), "rv": (128, 64)}
            for nm, (R, C) in shapes.items():
                f = nc.dram_tensor(nm + "f", [R, C], F32,
                                   kind="ExternalInput").ap()
                bq = nc.dram_tensor(nm + "b", [R, C], BF16).ap()
                if C == 128 and R % (128 * 8) == 0:
                    jobs.append((f.rearrange("(a b) m -> a (b m)", b=8),
                                 bq.rearrange("(a b) m -> a (b m)", b=8)))
                else:
                    jobs.append((f, bq))
                blobs[nm + "b"] = bq
            for nm, shp in [("bq", [128, 8]), ("bk", [128, 8]),
                            ("bv", [1024]), ("bo", [1024])]:
                blobs[nm] = nc.dram_tensor(nm, shp, F32,
                                           kind="ExternalInput").ap()
            blobs["msk"] = nc.dram_tensor("msk", [128, 2048], BF16,
                                          kind="ExternalInput").ap()
            emit_weight_prep(K, es, jobs)
            emit_l1_attn(K, es, x_in, x_out, blobs)
        elif name == "l3_attn":
            blobs = {}
            jobs = []
            shapes = {"wqk": (8192, 128), "wv": (1024, 512),
                      "wo": (512, 1024)}
            for nm, (R, C) in shapes.items():
                f = nc.dram_tensor(nm + "f", [R, C], F32,
                                   kind="ExternalInput").ap()
                bq = nc.dram_tensor(nm + "b", [R, C], BF16).ap()
                if C < 512:
                    jobs.append((f.rearrange("(a b) m -> a (b m)", b=8),
                                 bq.rearrange("(a b) m -> a (b m)", b=8)))
                else:
                    jobs.append((f, bq))
                blobs[nm + "b"] = bq
            blobs["bqk"] = nc.dram_tensor("bqk", [128, 8], F32,
                                          kind="ExternalInput").ap()
            blobs["qsc"] = nc.dram_tensor("qsc", [128, 1], F32,
                                          kind="ExternalInput").ap()
            blobs["bv"] = nc.dram_tensor("bv", [512], F32,
                                         kind="ExternalInput").ap()
            blobs["bo"] = nc.dram_tensor("bo", [1024], F32,
                                         kind="ExternalInput").ap()
            blobs["gam"] = nc.dram_tensor("gam", [1024], F32,
                                          kind="ExternalInput").ap()
            blobs["bet"] = nc.dram_tensor("bet", [1024], F32,
                                          kind="ExternalInput").ap()
            emit_weight_prep(K, es, jobs)
            emit_l3_attn(K, es, x_in, x_out, blobs)
        elif name == "l2_attn":
            blobs = {}
            jobs = []
            shapes = {"wq": (8192, 128), "wk": (8192, 128),
                      "wv0": (1024, 512), "wv1": (1024, 512),
                      "wo": (1024, 1024)}
            for nm, (R, C) in shapes.items():
                f = nc.dram_tensor(nm + "f", [R, C], F32,
                                   kind="ExternalInput").ap()
                bq = nc.dram_tensor(nm + "b", [R, C], BF16).ap()
                if C < 512:
                    jobs.append((f.rearrange("(a b) m -> a (b m)", b=8),
                                 bq.rearrange("(a b) m -> a (b m)", b=8)))
                else:
                    jobs.append((f, bq))
                blobs[nm + "b"] = bq
            blobs["wvT0"] = blobs.pop("wv0b")
            blobs["wvT1"] = blobs.pop("wv1b")
            blobs["woT"] = blobs.pop("wob")
            for nm, shp in [("bq", [128, 8]), ("bk", [128, 8]),
                            ("sb1c", [128, 4]), ("pb1c", [128, 12]),
                            ("pb2r", [1, 48]), ("sb2r", [1, 3]),
                            ("bv", [1024]), ("bo", [1024]),
                            ("sw1T", [4096, 128]),
                            ("pw1T0", [4096, 128]), ("pw1T1", [4096, 128]),
                            ("pw1T2", [4096, 128]), ("l2w", [512, 51])]:
                blobs[nm] = nc.dram_tensor(nm, shp, F32,
                                           kind="ExternalInput").ap()
            blobs["mask4"] = nc.dram_tensor("mask4", [128, 2048], BF16,
                                            kind="ExternalInput").ap()
            emit_weight_prep(K, es, jobs)
            emit_l2_attn(K, es, x_in, x_out, blobs)
        elif name in ("l2_ffn", "l3_ffn"):
            kind = "l2f" if name == "l2_ffn" else "l3f"
            NE = 17 if kind == "l2f" else 18
            blobs = {}
            jobs = []
            fup = nc.dram_tensor("WUPf", [NE * 8192, 128], F32,
                                 kind="ExternalInput").ap()
            bup = nc.dram_tensor("WUPb", [NE * 8192, 128], BF16).ap()
            jobs.append((fup.rearrange("(a b) m -> a (b m)", b=8),
                         bup.rearrange("(a b) m -> a (b m)", b=8)))
            fdn = nc.dram_tensor("WDNf", [NE * 512, 1024], F32,
                                 kind="ExternalInput").ap()
            bdn = nc.dram_tensor("WDNb", [NE * 512, 1024], BF16).ap()
            jobs.append((fdn, bdn))
            blobs["WUP"] = bup
            blobs["WDN"] = bdn
            blobs["gwT"] = nc.dram_tensor("gwT", [1024, 16], F32,
                                          kind="ExternalInput").ap()
            blobs["gb"] = nc.dram_tensor("gb", [16], F32,
                                         kind="ExternalInput").ap()
            emit_weight_prep(K, es, jobs)
            emit_moe_ffn(K, es, x_in, x_out, blobs, kind)
        elif name.startswith("l1_attn_upto"):
            blobs = {}
            jobs = []
            shapes = {"wq": (8192, 128), "wk": (8192, 128),
                      "wv": (1024, 1024), "wo": (1024, 1024),
                      "tk": (128, 32), "rv": (128, 64)}
            for nm, (R, C) in shapes.items():
                f = nc.dram_tensor(nm + "f", [R, C], F32,
                                   kind="ExternalInput").ap()
                bq = nc.dram_tensor(nm + "b", [R, C], BF16).ap()
                if C == 128 and R % (128 * 8) == 0:
                    jobs.append((f.rearrange("(a b) m -> a (b m)", b=8),
                                 bq.rearrange("(a b) m -> a (b m)", b=8)))
                else:
                    jobs.append((f, bq))
                blobs[nm + "b"] = bq
            for nm, shp in [("bq", [128, 8]), ("bk", [128, 8]),
                            ("bv", [1024]), ("bo", [1024])]:
                blobs[nm] = nc.dram_tensor(nm, shp, F32,
                                           kind="ExternalInput").ap()
            blobs["msk"] = nc.dram_tensor("msk", [128, 2048], BF16,
                                          kind="ExternalInput").ap()
            emit_weight_prep(K, es, jobs)
            emit_l1_attn(K, es, x_in, x_out, blobs,
                         upto=int(name.split("upto")[1]))
        elif name.startswith("l0_attn_upto"):
            emit_l0_attn(K, es, x_in, x_out, upto=int(name.split("upto")[1]))
        else:
            raise ValueError(name)
    return nc


_BLOCKS = ["l0_attn", "l0_ffn", "l1_attn", "l1_ffn",
           "l2_attn", "l2_ffn", "l3_attn", "l3_ffn"]


def kernel(**inputs):
    cst = _const_blob()
    cur = np.ascontiguousarray(inputs["L0_attn_x"], dtype=np.float32)
    for name in _BLOCKS:
        nc = build_single_block_nc(name)
        in_maps = []
        for c in range(NCORES):
            if name == "l1_attn":
                xi = np.ascontiguousarray(
                    cur[:, c * 256:(c + 1) * 256, :].transpose(1, 0, 2)
                    .reshape(TOK, D))
            else:
                xi = np.ascontiguousarray(
                    cur[2 * c:2 * c + 2].reshape(TOK, D))
            m = {"x_in": xi, "cst": cst}
            m.update(host_weight_maps(name, inputs, c))
            in_maps.append(m)
        res = run_bass_kernel_spmd(nc, in_maps, core_ids=list(range(NCORES)))
        if name == "l1_attn":
            nxt = np.empty((16, S, D), np.float32)
            for c in range(NCORES):
                nxt[:, c * 256:(c + 1) * 256, :] = (
                    np.asarray(res.results[c]["x_out"])
                    .reshape(256, 16, D).transpose(1, 0, 2))
        else:
            nxt = np.stack([np.asarray(res.results[c]["x_out"])
                            .reshape(2, S, D)
                            for c in range(NCORES)]).reshape(16, S, D)
        cur = nxt
    return cur


def host_weight_maps(which, inputs, core):
    if which == "l0_attn":
        return {}
    if which == "l0_ffn":
        def tile_blob(w, trans=True):
            wt = np.ascontiguousarray(w.T if trans else w)
            Kdim, M = wt.shape
            KC, MC = Kdim // 128, M // 128
            t = wt.reshape(KC, 128, MC, 128).transpose(2, 0, 1, 3)
            return np.ascontiguousarray(t.reshape(MC * KC * 128, 128))
        return {
            "w1f": tile_blob(inputs["L0_ffn_w1_w"]),
            "w2f": tile_blob(inputs["L0_ffn_w2_w"]),
            "h1f": tile_blob(inputs["L0_ffn_h1_w"]),
            "h2f": tile_blob(inputs["L0_ffn_h2_w"]),
            "w3f": tile_blob(inputs["L0_ffn_w3_w"]),
            "s1": np.ascontiguousarray(
                inputs["L0_ffn_w1_s"].reshape(-1, 128).T),
            "s2": np.ascontiguousarray(
                inputs["L0_ffn_w2_s"].reshape(-1, 128).T),
            "sh1": np.ascontiguousarray(
                inputs["L0_ffn_h1_s"].reshape(-1, 128).T),
            "sh2": np.ascontiguousarray(
                inputs["L0_ffn_h2_s"].reshape(-1, 128).T),
            "s3": inputs["L0_ffn_w3_s"],
        }
    if which == "l1_ffn":
        def tile_blob(w):
            wt = np.ascontiguousarray(w.T)
            Kdim, M = wt.shape
            KC, MC = Kdim // 128, M // 128
            t = wt.reshape(KC, 128, MC, 128).transpose(2, 0, 1, 3)
            return np.ascontiguousarray(t.reshape(MC * KC * 128, 128))
        return {
            "wgf": tile_blob(inputs["L1_ffn_w_gate"]),
            "wuf": tile_blob(inputs["L1_ffn_w_up"]),
            "wdf": tile_blob(inputs["L1_ffn_w_down"]),
        }
    if which == "l1_attn":
        def tile_blob(w):
            wt = np.ascontiguousarray(w.T)
            Kdim, M = wt.shape
            KC, MC = Kdim // 128, M // 128
            t = wt.reshape(KC, 128, MC, 128).transpose(2, 0, 1, 3)
            return np.ascontiguousarray(t.reshape(MC * KC * 128, 128))
        i = inputs
        ipw = i["L1_attn_in_proj_weight"]
        ipb = i["L1_attn_in_proj_bias"]
        tabk = i["L1_attn_rel_k_table"]
        tabv = i["L1_attn_rel_v_table"]
        tkf = np.zeros((128, 32), np.float32)
        tkf[0:64, 0:31] = tabk[1:32].T
        tkf[64:128, 0:31] = tabk[1:32].T
        rvf = np.zeros((128, 64), np.float32)
        rvf[0:31] = tabv[1:32]
        m = np.zeros((128, 16, 8, 16), np.float32)
        for c in range(8):
            m[c * 16:(c + 1) * 16, :, c, :] = 1.0
        return {
            "wqf": tile_blob(ipw[0:1024]),
            "wkf": tile_blob(ipw[1024:2048]),
            "wvf": np.ascontiguousarray(ipw[2048:3072].T),
            "wof": np.ascontiguousarray(i["L1_attn_out_w"].T),
            "tkf": np.ascontiguousarray(tkf),
            "rvf": rvf,
            "bq": np.ascontiguousarray(ipb[0:1024].reshape(8, 128).T),
            "bk": np.ascontiguousarray(ipb[1024:2048].reshape(8, 128).T),
            "bv": np.ascontiguousarray(ipb[2048:3072]),
            "bo": np.ascontiguousarray(i["L1_attn_out_b"]),
            "msk": m.reshape(128, 2048).astype(BF16NP),
        }
    if which == "l3_attn":
        def tile_blob_nt(wt):
            wt = np.ascontiguousarray(wt)
            Kdim, M = wt.shape
            KC, MC = Kdim // 128, M // 128
            t = wt.reshape(KC, 128, MC, 128).transpose(2, 0, 1, 3)
            return np.ascontiguousarray(t.reshape(MC * KC * 128, 128))
        H, DH = 8, 64
        perm = np.concatenate([
            np.arange(1024).reshape(H, DH, 2)[:, :, 0].reshape(-1),
            np.arange(1024).reshape(H, DH, 2)[:, :, 1].reshape(-1)])
        wqk = inputs["L3_attn_W_qk"][:, perm]
        bqk = inputs["L3_attn_b_qk"][perm]
        return {
            "wqkf": tile_blob_nt(wqk),
            "wvf": np.ascontiguousarray(inputs["L3_attn_W_v"]),
            "wof": np.ascontiguousarray(inputs["L3_attn_W_out"]),
            "bqk": np.ascontiguousarray(bqk.reshape(8, 128).T),
            "qsc": np.zeros((128, 1), np.float32),
            "bv": inputs["L3_attn_b_v"],
            "bo": inputs["L3_attn_b_out"],
            "gam": inputs["L3_attn_ln_gamma"],
            "bet": inputs["L3_attn_ln_beta"],
        }
    if which == "l2_attn":
        def _chunkcols(wt):
            wt = np.ascontiguousarray(wt)
            return np.ascontiguousarray(
                wt.reshape(1024, 4, 128).transpose(1, 0, 2)
                .reshape(4 * 1024, 128))
        def tile_blob(w):
            wt = np.ascontiguousarray(w.T)
            Kd, M = wt.shape
            KC, MC = Kd // 128, M // 128
            t = wt.reshape(KC, 128, MC, 128).transpose(2, 0, 1, 3)
            return np.ascontiguousarray(t.reshape(MC * KC * 128, 128))
        i = inputs
        mask4 = np.zeros((128, 2048), np.float32)
        for dlt in range(4):
            kk = np.arange(128)[:, None]
            qq = np.arange(512)[None, :]
            mask4[:, dlt * 512:(dlt + 1) * 512] = (
                qq - kk >= dlt * 128).astype(np.float32)
        l2w = np.concatenate([
            np.ascontiguousarray(i["L2_attn_sw2"].T),
            np.ascontiguousarray(i["L2_attn_pw2"][0].T),
            np.ascontiguousarray(i["L2_attn_pw2"][1].T),
            np.ascontiguousarray(i["L2_attn_pw2"][2].T)], axis=1)
        return {
            "wqf": tile_blob(i["L2_attn_wq"]),
            "wkf": tile_blob(i["L2_attn_wk"]),
            "wv0f": np.ascontiguousarray(i["L2_attn_wv"].T[:, 0:512]),
            "wv1f": np.ascontiguousarray(i["L2_attn_wv"].T[:, 512:1024]),
            "wof": np.ascontiguousarray(i["L2_attn_wo"].T),
            "bq": np.ascontiguousarray(i["L2_attn_bq"].reshape(8, 128).T),
            "bk": np.ascontiguousarray(i["L2_attn_bk"].reshape(8, 128).T),
            "bv": i["L2_attn_bv"], "bo": i["L2_attn_bo"],
            "sb1c": np.ascontiguousarray(
                i["L2_attn_sb1"].reshape(4, 128).T),
            "pb1c": np.ascontiguousarray(
                i["L2_attn_pb1"].reshape(3, 4, 128)
                .transpose(2, 0, 1).reshape(128, 12)),
            "pb2r": np.ascontiguousarray(
                i["L2_attn_pb2"].reshape(1, 48)),
            "sb2r": np.ascontiguousarray(i["L2_attn_sb2"].reshape(1, 3)),
            "sw1T": _chunkcols(i["L2_attn_sw1"].T),
            "pw1T0": _chunkcols(i["L2_attn_pw1"][0].T),
            "pw1T1": _chunkcols(i["L2_attn_pw1"][1].T),
            "pw1T2": _chunkcols(i["L2_attn_pw1"][2].T),
            "l2w": np.ascontiguousarray(l2w),
            "mask4": mask4.astype(BF16NP),
        }
    if which in ("l2_ffn", "l3_ffn"):
        def tile_nt(wt):
            wt = np.ascontiguousarray(wt)
            Kd, M = wt.shape
            KC, MC = Kd // 128, M // 128
            t = wt.reshape(KC, 128, MC, 128).transpose(2, 0, 1, 3)
            return np.ascontiguousarray(t.reshape(MC * KC * 128, 128))
        i = inputs
        if which == "l2_ffn":
            wgs = [i["L2_ffn_experts_wg"][e] for e in range(16)] +                   [i["L2_ffn_shared_wg"]]
            wus = [i["L2_ffn_experts_wu"][e] for e in range(16)] +                   [i["L2_ffn_shared_wu"]]
            wds = [i["L2_ffn_experts_wd"][e] for e in range(16)] +                   [i["L2_ffn_shared_wd"]]
            gw, gb = i["L2_ffn_gate_w"], i["L2_ffn_gate_b"]
        else:
            wgs = [i["L3_ffn_w_gate"][e] for e in range(16)] +                   [i["L3_ffn_sw_gate"][:, 0:512],
                   i["L3_ffn_sw_gate"][:, 512:1024]]
            wus = [i["L3_ffn_w_up"][e] for e in range(16)] +                   [i["L3_ffn_sw_up"][:, 0:512],
                   i["L3_ffn_sw_up"][:, 512:1024]]
            wds = [i["L3_ffn_w_down"][e] for e in range(16)] +                   [i["L3_ffn_sw_down"][0:512, :],
                   i["L3_ffn_sw_down"][512:1024, :]]
            gw, gb = i["L3_ffn_gate_kernel"], i["L3_ffn_gate_bias"]
        WUP = np.concatenate(
            [np.concatenate([tile_nt(wg), tile_nt(wu)], axis=0)
             for wg, wu in zip(wgs, wus)], axis=0)
        WDN = np.concatenate([np.ascontiguousarray(wd) for wd in wds],
                             axis=0)
        return {"WUPf": WUP, "WDNf": WDN,
                "gwT": np.ascontiguousarray(gw),
                "gb": np.ascontiguousarray(gb)}
    raise ValueError(which)
